# version 09
# baseline (speedup 1.0000x reference)
import numpy as np
import ml_dtypes
from contextlib import ExitStack

import concourse.bass as bass
import concourse.mybir as mybir
from concourse.bass_utils import run_bass_kernel_spmd

F32 = mybir.dt.float32
BF16 = mybir.dt.bfloat16
I8 = mybir.dt.int8
AF = mybir.ActivationFunctionType
ALU = mybir.AluOpType
AX = mybir.AxisListType
bfdt = ml_dtypes.bfloat16

B, S, H = 2, 8192, 2048
NH, D = 16, 128
TPC = 2048
NCH = TPC // 128
NEU = 4
INTER = 8192
DSCALE = float(D) ** -0.5
LN_EPS = 1e-5

TAPS = ()
PROJ_BARRIER = 0
STOP_AFTER = None


class _Eng:
    def __init__(self, nc, obj, name, sem, is_dma):
        self.nc, self.obj, self.name = nc, obj, name
        self.sem, self.is_dma = sem, is_dma
        self.count = 0
        self.last = None
        self.last_inced = True
        self.hw = {}

    def token(self):
        if self.last is None:
            return None
        if not self.last_inced:
            self.last.then_inc(self.sem, 1)
            self.count += 1
            self.last_inced = True
        return (self, self.count)

    def wait_tok(self, tok):
        if tok is None:
            return
        eng, val = tok
        if val <= 0 or eng is self:
            return
        if self.hw.get(eng.name, 0) >= val:
            return
        self.obj.wait_ge(eng.sem, val)
        self.hw[eng.name] = val


class Res:
    __slots__ = ("name", "prod", "readers")
    def __init__(self, name):
        self.name = name
        self.prod = None
        self.readers = {}

    def deps_for_read(self):
        return [self.prod]

    def deps_for_write(self):
        return [self.prod] + list(self.readers.values())


class Chan:
    __slots__ = ("name", "sem", "count")
    def __init__(self, name, sem):
        self.name, self.sem, self.count = name, sem, 0


class KB:
    def __init__(self, nc, ctx):
        self.nc = nc
        self.ctx = ctx
        sem = lambda n: ctx.enter_context(nc.semaphore(n))
        self.PE = _Eng(nc, nc.tensor, "PE", sem("sPE"), False)
        self.DV = _Eng(nc, nc.vector, "DV", sem("sDV"), False)
        self.AC = _Eng(nc, nc.scalar, "AC", sem("sAC"), False)
        self.SY = _Eng(nc, nc.sync, "SY", sem("sSY"), True)
        self.GP = _Eng(nc, nc.gpsimd, "GP", sem("sGP"), False)
        self.CC = sem("sCC")
        self.cc_count = 0
        self.engs = dict(PE=self.PE, DV=self.DV, AC=self.AC, GP=self.GP)
        self.chans = {}

    def chan(self, name):
        if name not in self.chans:
            self.chans[name] = Chan(
                name, self.ctx.enter_context(self.nc.semaphore("ch_" + name)))
        return self.chans[name]

    def _wait_deps(self, eng, deps):
        for d in deps:
            if d is None:
                continue
            if isinstance(d, tuple) and d[0] == "m":
                src = d[1]
                if src is eng and eng.name == "PE":
                    continue
                d = src.token()
            src = d[0]
            if src is eng:
                if eng.name == "PE":
                    continue
                if eng.hw.get(eng.name, 0) >= d[1]:
                    continue
                eng.obj.wait_ge(eng.sem, d[1])
                eng.hw[eng.name] = d[1]
                continue
            eng.wait_tok(d)

    def _freeze(self, res_list):
        for r in res_list:
            p = r.prod
            if isinstance(p, tuple) and p[0] == "m":
                r.prod = p[1].token()
            for k, v in list(r.readers.items()):
                if isinstance(v, tuple) and v[0] == "m":
                    r.readers[k] = v[1].token()

    def op(self, eng_name, fn, reads=(), writes=()):
        eng = self.engs[eng_name]
        self._freeze(reads)
        self._freeze(writes)
        deps = []
        for r in reads:
            deps.extend(r.deps_for_read())
        for w in writes:
            deps.extend(w.deps_for_write())
        self._wait_deps(eng, deps)
        instr = fn(eng.obj)
        eng.last = instr
        eng.last_inced = False
        for w in writes:
            w.prod = ("m", eng)
            w.readers = {}
        for r in reads:
            r.readers[eng.name] = ("m", eng)
        return instr

    def dma(self, q_name, dst_ap, src_ap, reads=(), writes=(), ch=None):
        eng = self.SY if q_name == "SY" else self.GP
        c = self.chan(ch if ch is not None else ("d" + q_name))
        self._freeze(reads)
        self._freeze(writes)
        deps = []
        for r in reads:
            deps.extend(r.deps_for_read())
        for w in writes:
            deps.extend(w.deps_for_write())
        self._wait_deps(eng, deps)
        instr = eng.obj.dma_start(dst_ap, src_ap)
        c.count += 16
        instr.then_inc(c.sem, 16)
        tok = (c, c.count)
        for w in writes:
            w.prod = tok
            w.readers = {}
        for r in reads:
            r.readers[c.name] = tok
        return tok

    def dma_t(self, q_name, dst_ap, src_ap, reads=(), writes=(), ch=None):
        eng = self.SY if q_name == "SY" else self.AC
        c = self.chan(ch if ch is not None else ("t" + q_name))
        self._freeze(reads)
        self._freeze(writes)
        deps = []
        for r in reads:
            deps.extend(r.deps_for_read())
        for w in writes:
            deps.extend(w.deps_for_write())
        self._wait_deps(eng, deps)
        instr = eng.obj.dma_start_transpose(dst_ap, src_ap)
        c.count += 16
        instr.then_inc(c.sem, 16)
        tok = (c, c.count)
        for w in writes:
            w.prod = tok
            w.readers = {}
        for r in reads:
            r.readers[c.name] = tok
        return tok

    def barrier(self):
        toks = [self.PE.token(), self.DV.token(), self.AC.token()]
        toks += [(c, c.count) for c in self.chans.values() if c.count]
        for e in (self.PE, self.DV, self.AC, self.SY, self.GP):
            for t in toks:
                if t is not None:
                    e.wait_tok(t)


class Build:
    def __init__(self, taps=(), stop_after=None):
        self.taps = tuple(taps)
        self.stop_after = stop_after
        self.nc = bass.Bass("TRN2", target_bir_lowering=False)
        self.base = ExitStack()
        self.kb = KB(self.nc, self.base)

    def sbuf(self, name, shape, dt, stack=None):
        t = (stack or self.base).enter_context(self.nc.sbuf_tensor(name, shape, dt))
        return (t, Res(name))

    def psum(self, name, shape, dt, stack=None):
        t = (stack or self.base).enter_context(self.nc.psum_tensor(name, shape, dt))
        return (t, Res(name))

    def dram(self, name, shape, dt, kind=None, addr_space=None):
        kw = {}
        if kind:
            kw["kind"] = kind
        if addr_space:
            kw["addr_space"] = addr_space
        t = self.nc.dram_tensor(name, shape, dt, **kw)
        return (t, Res(name))

    def bank(self, kind="F"):
        pool = self.banksF if kind == "F" else self.banksB
        i = self._rot[kind] % len(pool)
        self._rot[kind] += 1
        return pool[i]

    def build(self):
        kb = self.kb
        dr = self.dram
        self.x_r = dr("x_r", [128 + TPC, H], F32, "ExternalInput")
        self.out_r = dr("out_r", [TPC, H], F32, "ExternalOutput")
        for w in ("Wq", "Wk", "Wv", "Wg", "Wo"):
            setattr(self, w, dr(w, [H, H], F32, "ExternalInput"))
        self.WaT = dr("WaT", [NH, H], F32, "ExternalInput")
        self.WbT = dr("WbT", [NH, H], F32, "ExternalInput")
        self.ba_r = dr("ba_r", [NH, 1], F32, "ExternalInput")
        self.bb_r = dr("bb_r", [NH, 1], F32, "ExternalInput")
        for w in ("qcw_r", "kcw_r", "vcw_r"):
            setattr(self, w, dr(w, [128, 64], F32, "ExternalInput"))
        for w in ("qcb_r", "kcb_r", "vcb_r"):
            setattr(self, w, dr(w, [128, 16], F32, "ExternalInput"))
        self.lnw_r = dr("lnw_r", [128, 1], F32, "ExternalInput")
        self.lnb_r = dr("lnb_r", [128, 1], F32, "ExternalInput")
        self.wg8 = dr("wg8", [16, 128, INTER], I8, "ExternalInput")
        self.wu8 = dr("wu8", [16, 128, INTER], I8, "ExternalInput")
        self.wd8 = dr("wd8", [2, 8, 128, 8192], I8, "ExternalInput")
        self.sg_r = dr("sg_r", [128, 64], F32, "ExternalInput")
        self.su_r = dr("su_r", [128, 64], F32, "ExternalInput")
        self.sd_r = dr("sd_r", [128, 16], F32, "ExternalInput")
        self.idf = dr("idf", [128, 128], F32, "ExternalInput")
        self.idb = dr("idb", [128, 128], BF16, "ExternalInput")
        self.msu = dr("msu", [128, 128], F32, "ExternalInput")
        self.miu = dr("miu", [128, 128], F32, "ExternalInput")
        self.onesr = dr("onesr", [65, 128], F32, "ExternalInput")
        self.mj_r = dr("mj_r", [128, 6], F32, "ExternalInput")
        self.kT_s = dr("kT_s", [NH, 128, TPC], BF16)
        self.vT_s = dr("vT_s", [NH, 128, TPC], BF16)
        self.qT_s = dr("qT_s", [NH, 128, TPC], BF16)
        self.gT_s = dr("gT_s", [NH, 128, TPC], BF16)
        self.rows_d = dr("rows_d", [2, NH, TPC], F32)
        self.x1_s = dr("x1_s", [NCH, 128, H], F32)
        self.xn1_s = dr("xn1_s", [NCH, 128, H], BF16)
        self.cc_in = dr("cc_in", [NH, 2, 128, 128], BF16)
        self.cc_out = dr("cc_out", [4, NH, 2, 128, 128], BF16)

        self.A1 = [self.sbuf(f"A1_{i}", [128, TPC], BF16) for i in range(16)]
        self.A2 = [self.sbuf(f"A2_{i}", [128, TPC], BF16) for i in range(16)]
        self.banksF = [self.psum(f"pf{i}", [128, 512], F32) for i in range(6)]
        self.psumB_stack = ExitStack()
        self.banksB = [self.psum(f"pb{i}", [128, 1024], BF16,
                                 stack=self.psumB_stack) for i in range(2)]
        self._rot = {"F": 0, "B": 0}
        self.c_idf = self.sbuf("c_idf", [128, 128], F32)
        self.c_idb = self.sbuf("c_idb", [128, 128], BF16)
        self.c_msu = self.sbuf("c_msu", [128, 128], F32)
        self.c_miu = self.sbuf("c_miu", [128, 128], F32)
        self.c_ones = self.sbuf("c_ones", [65, 128], F32)
        self.c_mj = self.sbuf("c_mj", [128, 6], F32)
        self.c_lnw = self.sbuf("c_lnw", [128, 1], F32)
        self.c_lnb = self.sbuf("c_lnb", [128, 1], F32)
        self.c_ba = self.sbuf("c_ba", [NH, 1], F32)
        self.c_bb = self.sbuf("c_bb", [NH, 1], F32)
        self.c_sg = self.sbuf("c_sg", [128, 64], F32)
        self.c_su = self.sbuf("c_su", [128, 64], F32)
        self.c_sd = self.sbuf("c_sd", [128, 16], F32)
        self.c_eps = self.sbuf("c_eps", [128, 1], F32)
        self.lncols = self.sbuf("lncols", [128, 8], F32)
        self.xnT_halo = self.sbuf("xnT_halo", [128, 48], BF16)

        self.scan_stack = ExitStack()
        ss = self.scan_stack
        self.c_cvw = {k: self.sbuf(f"c_{k}w", [128, 64], F32, ss)
                      for k in ("q", "k", "v")}
        self.c_cvb = {k: self.sbuf(f"c_{k}b", [128, 16], F32, ss)
                      for k in ("q", "k", "v")}
        self.c_ones16 = self.sbuf("c_ones16", [NH, 128], F32, ss)
        self.colpack = self.sbuf("colpack", [128, NCH * 6 * NH], F32, ss)
        self.S0T = [self.sbuf(f"S0T_{h}", [128, 128], F32, ss) for h in range(NH)]
        self.S0b = [self.sbuf(f"S0b_{h}", [128, 128], BF16, ss) for h in range(NH)]

        ns = self.nc.named_scope
        with ns("p0"):
            self.p0_consts()
        if self.stop_after == "consts":
            return self.finish()
        with ns("p1_xn"):
            self.p1_xn()
        if self.stop_after == "xn":
            return self.finish()
        if self.stop_after == "xnT_dump":
            for d in range(16):
                self.kb.dma("SY", self.qT_s[0][d], self.A1[d][0][:],
                            reads=[self.A1[d][1]], writes=[self.qT_s[1]],
                            ch="tap")
            self.kb.dma("SY", self.gT_s[0][0][:, 0:48],
                        self.xnT_halo[0][:],
                        reads=[self.xnT_halo[1]], writes=[self.gT_s[1]],
                        ch="tap")
            return self.finish()
        with ns("p3_ab"):
            self.p3_ab()
        with ns("projK"):
            self.proj("Wk", "k", self.kT_s, silu=True, scale=DSCALE)
        with ns("projV"):
            self.proj("Wv", "v", self.vT_s, silu=True, scale=1.0)
        if self.stop_after == "proj_kv":
            return self.finish()
        if self.stop_after == "phase1":
            with ns("p4_phase1"):
                self.p4_phase1()
            return self.finish()
        with ns("p4_projqg"):
            gp4 = self.p4_gen()
            def projs():
                yield from self.proj_gen("Wq", "q", self.qT_s,
                                         silu=True, scale=1.0)
                yield from self.proj_gen("Wg", None, self.gT_s,
                                         silu=False, scale=1.0)
            gpr = projs()
            alive = [gp4, gpr]
            while alive:
                nxt = []
                for g in alive:
                    try:
                        next(g)
                        nxt.append(g)
                    except StopIteration:
                        pass
                alive = nxt
        if self.stop_after == "proj_qg":
            return self.finish()
        with ns("p6_compose"):
            self.p6_compose()
        if self.stop_after == "compose":
            return self.finish()
        with ns("p7_phase2"):
            self.p7_phase2()
        if self.stop_after == "phase2":
            return self.finish()
        with ns("p7b_epi"):
            self.p7b_epilogue()
        if self.stop_after == "epilogue":
            return self.finish()
        self.kb.barrier()
        self.scan_stack.close()
        with ns("p8_ffn"):
            self.p8_ffn()
        return self.finish()

    def p0_consts(self):
        kb = self.kb
        loads = [
            (self.c_idf, self.idf), (self.c_idb, self.idb),
            (self.c_msu, self.msu), (self.c_miu, self.miu),
            (self.c_ones, self.onesr), (self.c_mj, self.mj_r),
            (self.c_cvw["q"], self.qcw_r), (self.c_cvw["k"], self.kcw_r),
            (self.c_cvw["v"], self.vcw_r),
            (self.c_cvb["q"], self.qcb_r), (self.c_cvb["k"], self.kcb_r),
            (self.c_cvb["v"], self.vcb_r),
            (self.c_lnw, self.lnw_r), (self.c_lnb, self.lnb_r),
            (self.c_ba, self.ba_r), (self.c_bb, self.bb_r),
            (self.c_sg, self.sg_r), (self.c_su, self.su_r), (self.c_sd, self.sd_r),
        ]
        ctoks = []
        for (st, sr), (dt_, drr) in loads:
            ctoks.append((sr, kb.dma("SY", st[:], dt_[:], reads=[drr],
                                     writes=[sr], ch="c")))
        final = ctoks[-1][1]
        for sr, _ in ctoks:
            sr.prod = final
        kb.op("DV", lambda e: e.memset(self.c_ones16[0][:], 1.0),
              writes=[self.c_ones16[1]])
        kb.op("DV", lambda e: e.memset(self.c_eps[0][:], LN_EPS),
              writes=[self.c_eps[1]])

    def emit_ln(self, src, dst, width, rows=128, cols=None, scr=None):
        kb = self.kb
        ct, cr = cols or self.lncols
        sct, scr_r = scr
        st_, sr_ = src
        dt_, drr = dst
        r = slice(0, rows)
        kb.op("DV", lambda e: e.tensor_reduce(ct[r, 0:1], st_[r, :], axis=AX.X,
                                              op=ALU.add),
              reads=[sr_], writes=[cr])
        kb.op("AC", lambda e: e.activation(sct[r, :], st_[r, :], AF.Square,
                                           accum_out=ct[r, 1:2]),
              reads=[sr_], writes=[scr_r, cr])
        kb.op("DV", lambda e: e.tensor_scalar(ct[r, 2:3], ct[r, 0:1], 1.0 / width,
                                              None, op0=ALU.mult),
              reads=[cr], writes=[cr])
        kb.op("DV", lambda e: e.tensor_tensor(ct[r, 3:4], ct[r, 2:3], ct[r, 2:3],
                                              op=ALU.mult),
              reads=[cr], writes=[cr])
        kb.op("DV", lambda e: e.tensor_scalar(ct[r, 4:5], ct[r, 1:2], 1.0 / width,
                                              ct[r, 3:4], op0=ALU.mult,
                                              op1=ALU.subtract),
              reads=[cr], writes=[cr])
        kb.op("AC", lambda e: e.activation(ct[r, 5:6], ct[r, 4:5], AF.Ln,
                                           bias=self.c_eps[0][r, 0:1]),
              reads=[cr, self.c_eps[1]], writes=[cr])
        kb.op("AC", lambda e: e.activation(ct[r, 6:7], ct[r, 5:6], AF.Exp,
                                           scale=-0.5),
              reads=[cr], writes=[cr])
        kb.op("DV", lambda e: e.tensor_scalar(dt_[r, :], st_[r, :], ct[r, 2:3],
                                              ct[r, 6:7], op0=ALU.subtract,
                                              op1=ALU.mult),
              reads=[sr_, cr], writes=[drr])

    def p1_xn(self):
        kb = self.kb
        self.p1_stack = ExitStack()
        xb = [self.sbuf(f"xb{i}", [128, H], F32, self.p1_stack) for i in range(2)]
        xn = [self.sbuf(f"xn{i}", [128, H], F32, self.p1_stack) for i in range(2)]
        lnscr = self.sbuf("lnscr1", [128, H], F32, self.p1_stack)
        xr, xres = self.x_r
        for ci in range(NCH + 1):
            tc = ci - 1
            bx, nx = xb[ci % 2], xn[ci % 2]
            kb.dma("SY", bx[0][:], xr[ci * 128:(ci + 1) * 128, :],
                   reads=[xres], writes=[bx[1]], ch=f"x{ci % 2}")
            self.emit_ln(bx, nx, H, scr=lnscr)
            for d in range(16):
                bk = self.bank("F")
                kb.op("PE", lambda e, bk=bk, nx=nx, d=d: e.transpose(
                    bk[0][:, :128], nx[0][:, d * 128:(d + 1) * 128],
                    self.c_idf[0][:]),
                    reads=[nx[1], self.c_idf[1]], writes=[bk[1]])
                if ci == 0:
                    kb.op("AC", lambda e, bk=bk, d=d: e.copy(
                        self.xnT_halo[0][:, d * 3:d * 3 + 3],
                        bk[0][:, 125:128]),
                        reads=[bk[1]], writes=[self.xnT_halo[1]])
                else:
                    kb.op("AC", lambda e, bk=bk, d=d, tc=tc: e.copy(
                        self.A1[d][0][:, tc * 128:(tc + 1) * 128],
                        bk[0][:, :128]),
                        reads=[bk[1]], writes=[self.A1[d][1]])
        kb.barrier()
        self.p1_stack.close()

    def proj(self, wname, conv, spill, silu, scale):
        for _ in self.proj_gen(wname, conv, spill, silu, scale):
            pass

    def proj_gen(self, wname, conv, spill, silu, scale):
        kb = self.kb
        pst = ExitStack()
        wd, wres = getattr(self, wname)
        spt, spres = spill
        krs = [self.sbuf(f"kraw_{wname}", [128, TPC + 3], BF16, pst)]
        cys = [self.sbuf(f"convy_{wname}", [128, TPC], F32, pst)]
        cts = [self.sbuf(f"convt_{wname}", [128, TPC], BF16, pst)]
        kss = [self.sbuf(f"ksil_{wname}", [128, TPC], BF16, pst)]
        for i in range(16):
            tokw = kb.dma("GP", self.A2[i][0][:], wd[i * 128:(i + 1) * 128, :],
                          reads=[wres], writes=[self.A2[i][1]], ch="wgt")
        for i in range(16):
            self.A2[i][1].prod = tokw
        for j in range(16):
            kr, cy, ct_, ks = krs[0], cys[0], cts[0], kss[0]
            if PROJ_BARRIER & 1:
                kb.barrier()
            for tg in range(4):
                bk = self.bank("F")
                for i in range(16):
                    kb.op("PE", lambda e, bk=bk, i=i, j=j, tg=tg: e.matmul(
                        bk[0][:, :], self.A2[i][0][:, j * 128:(j + 1) * 128],
                        self.A1[i][0][:, tg * 512:(tg + 1) * 512],
                        start=(i == 0), stop=(i == 15)),
                        reads=[self.A2[i][1], self.A1[i][1]],
                        writes=([bk[1]] if i in (0, 15) else []))
                if PROJ_BARRIER & 2:
                    kb.barrier()
                if conv is not None:
                    kb.op("AC", lambda e, bk=bk, tg=tg: e.copy(
                        kr[0][:, 3 + tg * 512: 3 + (tg + 1) * 512], bk[0][:, :]),
                        reads=[bk[1]], writes=[kr[1]])
                    yield
                else:
                    kb.op("AC", lambda e, bk=bk, tg=tg: e.activation(
                        ks[0][:, tg * 512:(tg + 1) * 512], bk[0][:, :],
                        AF.Sigmoid),
                        reads=[bk[1]], writes=[ks[1]])
                yield
            if conv is not None:
                bh = self.bank("F")
                for i in range(16):
                    kb.op("PE", lambda e, bh=bh, i=i, j=j: e.matmul(
                        bh[0][:, :3], self.A2[i][0][:, j * 128:(j + 1) * 128],
                        self.xnT_halo[0][:, i * 3:i * 3 + 3],
                        start=(i == 0), stop=(i == 15)),
                        reads=[self.A2[i][1], self.xnT_halo[1]],
                        writes=([bh[1]] if i in (0, 15) else []))
                kb.op("AC", lambda e, bh=bh: e.copy(kr[0][:, 0:3], bh[0][:, :3]),
                      reads=[bh[1]], writes=[kr[1]])
                yield
                if PROJ_BARRIER & 4:
                    kb.barrier()
                cw, cwres = self.c_cvw[conv]
                cb, cbres = self.c_cvb[conv]
                kb.op("DV", lambda e, j=j: e.tensor_scalar(
                    cy[0][:, :], kr[0][:, 0:TPC], cw[:, j * 4:j * 4 + 1],
                    cb[:, j:j + 1], op0=ALU.mult, op1=ALU.add),
                    reads=[kr[1], cwres, cbres], writes=[cy[1]])
                for jj in range(1, 4):
                    kb.op("AC", lambda e, j=j, jj=jj: e.activation(
                        ct_[0][:, :], kr[0][:, jj:jj + TPC], AF.Copy,
                        scale=cw[:, j * 4 + jj:j * 4 + jj + 1]),
                        reads=[kr[1], cwres], writes=[ct_[1]])
                    kb.op("DV", lambda e: e.tensor_tensor(
                        cy[0][:, :], cy[0][:, :], ct_[0][:, :], op=ALU.add),
                        reads=[cy[1], ct_[1]], writes=[cy[1]])
                if scale != 1.0:
                    kb.op("AC", lambda e: e.activation(cy[0][:, :], cy[0][:, :],
                                                       AF.Silu),
                          reads=[cy[1]], writes=[cy[1]])
                    kb.op("DV", lambda e: e.tensor_scalar(
                        ks[0][:, :], cy[0][:, :], scale, None, op0=ALU.mult),
                        reads=[cy[1]], writes=[ks[1]])
                else:
                    kb.op("AC", lambda e: e.activation(ks[0][:, :], cy[0][:, :],
                                                       AF.Silu),
                          reads=[cy[1]], writes=[ks[1]])
            if PROJ_BARRIER & 8:
                kb.barrier()
            kb.dma("SY", spt[j], ks[0][:, :], reads=[ks[1]], writes=[spres],
                   ch=f"sp_{wname}0")
            yield
        for par in range(2):
            c = kb.chan(f"sp_{wname}{par}")
            kb.SY.wait_tok((c, c.count))
        spres.prod = None
        pst.close()

    def p3_ab(self):
        kb = self.kb
        p3st = ExitStack()
        ss = p3st
        wab = self.sbuf("wab", [128, 2 * 16 * 16], BF16, ss)
        wst = ExitStack()
        watt = self.sbuf("watt", [NH, H], F32, wst)
        wbtt = self.sbuf("wbtt", [NH, H], F32, wst)
        kb.dma("SY", watt[0][:], self.WaT[0][:], reads=[self.WaT[1]],
               writes=[watt[1]], ch="wa")
        kb.dma("SY", wbtt[0][:], self.WbT[0][:], reads=[self.WbT[1]],
               writes=[wbtt[1]], ch="wb")
        for ab, wt in enumerate((watt, wbtt)):
            for c in range(16):
                bk = self.bank("F")
                kb.op("PE", lambda e, bk=bk, wt=wt, c=c: e.transpose(
                    bk[0][:, :16], wt[0][:, c * 128:(c + 1) * 128],
                    self.c_idf[0][0:16, 0:16]),
                    reads=[wt[1], self.c_idf[1]], writes=[bk[1]])
                kb.op("AC", lambda e, bk=bk, ab=ab, c=c: e.copy(
                    wab[0][:, (ab * 16 + c) * 16:(ab * 16 + c) * 16 + 16],
                    bk[0][:, :16]),
                    reads=[bk[1]], writes=[wab[1]])
        kb.barrier()
        wst.close()
        zst = ExitStack()
        laF = self.sbuf("laF", [NH, TPC], F32, zst)
        bsF = self.sbuf("bsF", [NH, TPC], F32, zst)
        lbF = self.sbuf("lbF", [NH, TPC], F32, zst)
        for ab in range(2):
            dst_s = laF if ab == 0 else bsF
            dst_l = laF if ab == 0 else lbF
            bias = self.c_ba if ab == 0 else self.c_bb
            for tg in range(4):
                bk = self.bank("F")
                for c in range(16):
                    kb.op("PE", lambda e, bk=bk, ab=ab, c=c, tg=tg: e.matmul(
                        bk[0][0:16, :],
                        wab[0][:, (ab * 16 + c) * 16:(ab * 16 + c) * 16 + 16],
                        self.A1[c][0][:, tg * 512:(tg + 1) * 512],
                        start=(c == 0), stop=(c == 15)),
                        reads=[wab[1], self.A1[c][1]],
                        writes=([bk[1]] if c in (0, 15) else []))
                kb.op("AC", lambda e, bk=bk, d=dst_s, b=bias, tg=tg: e.activation(
                    d[0][:, tg * 512:(tg + 1) * 512], bk[0][0:16, :], AF.Sigmoid,
                    bias=b[0][:, :]),
                    reads=[bk[1], bias[1]], writes=[dst_s[1]])
            kb.op("AC", lambda e, s=dst_s, l=dst_l: e.activation(
                l[0][:, :], s[0][:, :], AF.Ln),
                reads=[dst_s[1]], writes=[dst_l[1]])
        caF = self.sbuf("caF", [NH, TPC], F32, zst)
        sc16 = self.sbuf("sc16", [NH, 128], F32, zst)
        sc2 = self.sbuf("sc2", [NH, 2], F32, zst)
        latok = self.sbuf("latok", [128, 16], F32, zst)
        bstok = self.sbuf("bstok", [128, 16], F32, zst)
        catok = self.sbuf("catok", [128, 16], F32, zst)
        cash = self.sbuf("cash", [128, 16], F32, zst)
        ecash = self.sbuf("ecash", [128, 16], F32, zst)
        cp, cpr = self.colpack
        for tc in range(NCH):
            sl = slice(tc * 128, (tc + 1) * 128)
            base = tc * 6 * NH
            col = lambda q: cp[:, base + q * NH: base + (q + 1) * NH]
            b1 = self.bank("F")
            kb.op("PE", lambda e, b=b1, sl=sl: e.transpose(
                b[0][:, :16], laF[0][:, sl], self.c_idf[0][0:16, 0:16]),
                reads=[laF[1], self.c_idf[1]], writes=[b1[1]])
            kb.op("AC", lambda e, b=b1: e.copy(latok[0][:], b[0][:, :16]),
                  reads=[b1[1]], writes=[latok[1]])
            b2 = self.bank("F")
            kb.op("PE", lambda e, b=b2, sl=sl: e.transpose(
                b[0][:, :16], bsF[0][:, sl], self.c_idf[0][0:16, 0:16]),
                reads=[bsF[1], self.c_idf[1]], writes=[b2[1]])
            kb.op("AC", lambda e, b=b2: e.copy(bstok[0][:], b[0][:, :16]),
                  reads=[b2[1]], writes=[bstok[1]])
            kb.op("DV", lambda e: e.tensor_copy(col(0), bstok[0][:]),
                  reads=[bstok[1]], writes=[cpr])
            b3 = self.bank("F")
            kb.op("PE", lambda e, b=b3: e.matmul(
                b[0][:, 0:16], self.c_miu[0][:], latok[0][:],
                start=True, stop=True),
                reads=[self.c_miu[1], latok[1]], writes=[b3[1]])
            kb.op("AC", lambda e, b=b3: e.copy(catok[0][:], b[0][:, 0:16]),
                  reads=[b3[1]], writes=[catok[1]])
            kb.op("DV", lambda e: e.tensor_scalar(col(4), catok[0][:], -1.0,
                                                  None, op0=ALU.mult),
                  reads=[catok[1]], writes=[cpr])
            kb.op("AC", lambda e: e.activation(col(2), catok[0][:], AF.Exp),
                  reads=[catok[1]], writes=[cpr])
            kb.op("DV", lambda e: e.tensor_tensor(cash[0][:], catok[0][:],
                                                  latok[0][:], op=ALU.subtract),
                  reads=[catok[1], latok[1]], writes=[cash[1]])
            kb.op("AC", lambda e: e.activation(ecash[0][:], cash[0][:], AF.Exp),
                  reads=[cash[1]], writes=[ecash[1]])
            kb.op("DV", lambda e: e.tensor_tensor(cash[0][:], ecash[0][:],
                                                  bstok[0][:], op=ALU.mult),
                  reads=[ecash[1], bstok[1]], writes=[cash[1]])
            kb.op("DV", lambda e: e.tensor_scalar(col(1), cash[0][:], -1.0,
                                                  None, op0=ALU.mult),
                  reads=[cash[1]], writes=[cpr])
            b4 = self.bank("F")
            kb.op("PE", lambda e, b=b4: e.transpose(
                b[0][0:16, :128], catok[0][:], self.c_idf[0][:]),
                reads=[catok[1], self.c_idf[1]], writes=[b4[1]])
            kb.op("AC", lambda e, b=b4, sl=sl: e.copy(caF[0][:, sl],
                                                      b[0][0:16, :128]),
                  reads=[b4[1]], writes=[caF[1]])
            kb.op("DV", lambda e, sl=sl: e.tensor_reduce(
                sc2[0][:, 0:1], laF[0][:, sl], axis=AX.X, op=ALU.add),
                reads=[laF[1]], writes=[sc2[1]])
            kb.op("AC", lambda e, sl=sl: e.activation(
                sc16[0][:, :], caF[0][:, sl], AF.Exp, bias=sc2[0][:, 0:1],
                scale=-1.0),
                reads=[caF[1], sc2[1]], writes=[sc16[1]])
            b5 = self.bank("F")
            kb.op("PE", lambda e, b=b5: e.transpose(
                b[0][:, :16], sc16[0][:, :], self.c_idf[0][0:16, 0:16]),
                reads=[sc16[1], self.c_idf[1]], writes=[b5[1]])
            kb.op("AC", lambda e, b=b5: e.copy(col(3), b[0][:, :16]),
                  reads=[b5[1]], writes=[cpr])
            kb.op("AC", lambda e: e.activation(sc2[0][:, 1:2], sc2[0][:, 0:1],
                                               AF.Exp),
                  reads=[sc2[1]], writes=[sc2[1]])
            kb.op("DV", lambda e: e.tensor_scalar(sc16[0][:, :],
                                                  self.c_ones16[0][:],
                                                  sc2[0][:, 1:2], None,
                                                  op0=ALU.mult),
                  reads=[self.c_ones16[1], sc2[1]], writes=[sc16[1]])
            b6 = self.bank("F")
            kb.op("PE", lambda e, b=b6: e.transpose(
                b[0][:, :16], sc16[0][:, :], self.c_idf[0][0:16, 0:16]),
                reads=[sc16[1], self.c_idf[1]], writes=[b6[1]])
            kb.op("AC", lambda e, b=b6: e.copy(col(5), b[0][:, :16]),
                  reads=[b6[1]], writes=[cpr])
        r2F = bsF
        kb.op("DV", lambda e: e.tensor_tensor(r2F[0][:], caF[0][:], laF[0][:],
                                              op=ALU.subtract),
              reads=[caF[1], laF[1]], writes=[r2F[1]])
        kb.op("DV", lambda e: e.tensor_tensor(r2F[0][:], r2F[0][:], lbF[0][:],
                                              op=ALU.add),
              reads=[r2F[1], lbF[1]], writes=[r2F[1]])
        rd, rdr = self.rows_d
        kb.dma("SY", rd[0], r2F[0][:], reads=[r2F[1]], writes=[rdr],
               ch="sp_r")
        kb.dma("SY", rd[1], caF[0][:], reads=[caF[1]], writes=[rdr],
               ch="sp_r")
        kb.barrier()
        zst.close()
        p3st.close()

    def ccol(self, tc, q, h):
        return self.colpack[0][:, tc * 6 * NH + q * NH + h:
                               tc * 6 * NH + q * NH + h + 1]

    def p4_phase1(self):
        for _ in self.p4_gen():
            pass

    def p4_gen(self):
        kb = self.kb
        st = ExitStack()
        self.p4_stack = st
        PTa = [self.sbuf(f"PTa{h}", [128, 128], BF16, st) for h in range(NH)]
        BTa = [self.sbuf(f"BTa{h}", [128, 128], BF16, st) for h in range(NH)]
        rwt = self.sbuf("p1rows", [65, TPC], F32, st)
        S = []
        for p in range(2):
            d = {}
            d["rwb"] = ((rwt[0], 64 * p), rwt[1])
            d["khb"] = self.sbuf(f"p1kh{p}", [128, TPC // 2], BF16, st)
            d["vhb"] = self.sbuf(f"p1vh{p}", [128, TPC // 2], BF16, st)
            d["kkts"] = self.sbuf(f"s_kkts{p}", [128, 128], F32, st)
            d["rc"] = self.sbuf(f"s_rc{p}", [128, 128], F32, st)
            d["e1"] = self.sbuf(f"s_e1{p}", [128, 128], F32, st)
            d["lt"] = self.sbuf(f"s_lt{p}", [128, 128], BF16, st)
            d["kpr"] = self.sbuf(f"s_kpr{p}", [128, 128], BF16, st)
            d["rhs0"] = self.sbuf(f"s_rhs0{p}", [128, 256], F32, st)
            d["xb"] = [self.sbuf(f"s_xb{p}_{i}", [128, 256], BF16, st)
                       for i in range(2)]
            d["ptile"] = self.sbuf(f"s_ptile{p}", [128, 128], BF16, st)
            d["pcs"] = self.sbuf(f"s_pc{p}", [128, 128], BF16, st)
            S.append(d)
        cpr = self.colpack[1]
        for h in range(NH):
            kb.op("DV", lambda e, h=h: e.tensor_copy(PTa[h][0][:],
                                                     self.c_idb[0][:]),
                  reads=[self.c_idb[1]], writes=[PTa[h][1]])
            kb.op("DV", lambda e, h=h: e.memset(BTa[h][0][:], 0.0),
                  writes=[BTa[h][1]])
        rd, rdr = self.rows_d

        def load_head(h, d, half):
            hs = slice(half * (TPC // 2), (half + 1) * (TPC // 2))
            kb.dma("SY", d["khb"][0][:], self.kT_s[0][h][:, hs],
                   reads=[self.kT_s[1]], writes=[d["khb"][1]],
                   ch=f"kh{h % 2}")
            kb.dma("SY", d["vhb"][0][:], self.vT_s[0][h][:, hs],
                   reads=[self.vT_s[1]], writes=[d["vhb"][1]],
                   ch=f"vh{h % 2}")
            if half == 0:
                (rt, rb), rres = d["rwb"]
                kb.dma("SY", rt[rb:rb + 1, :], rd[0:1, h, :],
                       reads=[rdr], writes=[rres], ch=f"rw{h % 2}")

        def chunk1(h, tc, d):
            sl = slice(tc * 128, (tc + 1) * 128)
            slh = slice((tc % 8) * 128, (tc % 8 + 1) * 128)
            kT = (d["khb"][0][:, slh], d["khb"][1])
            vT = (d["vhb"][0][:, slh], d["vhb"][1])
            (rt, rb), rres = d["rwb"]
            r2row = (rt[rb:rb + 1, sl], rres)
            kkts, rc, e1 = d["kkts"], d["rc"], d["e1"]
            lt, kpr, rhs0 = d["lt"], d["kpr"], d["rhs0"]
            xb, ptile, pcs = d["xb"], d["ptile"], d["pcs"]
            aci = rc
            bk1 = self.bank("F")
            kb.op("PE", lambda e, b=bk1, kT=kT: e.matmul(
                b[0][:, :128], kT[0], kT[0], start=True, stop=True),
                reads=[kT[1]], writes=[bk1[1]])
            bk2 = self.bank("F")
            kb.op("PE", lambda e, b=bk2, r2=r2row, rb=rb: e.matmul(
                b[0][:, :128], self.c_ones[0][rb:rb + 1, :], r2[0],
                start=True, stop=True),
                reads=[self.c_ones[1], r2row[1]], writes=[bk2[1]])
            bt1 = self.bank("B")
            kb.op("PE", lambda e, b=bt1, kT=kT: e.transpose(
                b[0][:, :128], kT[0], self.c_idb[0][:]),
                reads=[kT[1], self.c_idb[1]], writes=[bt1[1]])
            bt2 = self.bank("B")
            kb.op("PE", lambda e, b=bt2, vT=vT: e.transpose(
                b[0][:, :128], vT[0], self.c_idb[0][:]),
                reads=[vT[1], self.c_idb[1]], writes=[bt2[1]])
            kb.op("DV", lambda e, b=bk1: e.tensor_tensor(
                kkts[0][:], b[0][:, :128], self.c_msu[0][:], op=ALU.mult),
                reads=[bk1[1], self.c_msu[1]], writes=[kkts[1]])
            kb.op("DV", lambda e, b=bk2, tc=tc, h=h: e.tensor_scalar(
                rc[0][:], b[0][:, :128], self.ccol(tc, 4, h), 0.0,
                op0=ALU.add, op1=ALU.min),
                reads=[bk2[1], cpr], writes=[rc[1]])
            kb.op("AC", lambda e: e.activation(e1[0][:], rc[0][:], AF.Exp),
                  reads=[rc[1]], writes=[e1[1]])
            kb.op("DV", lambda e: e.tensor_tensor(lt[0][:], e1[0][:],
                                                  kkts[0][:], op=ALU.mult),
                  reads=[e1[1], kkts[1]], writes=[lt[1]])
            kb.op("AC", lambda e, b=bt1, tc=tc, h=h: e.activation(
                kpr[0][:], b[0][:, :128], AF.Copy,
                scale=self.ccol(tc, 3, h)),
                reads=[bt1[1], cpr], writes=[kpr[1]])
            kb.op("AC", lambda e, b=bt2, tc=tc, h=h: e.activation(
                rhs0[0][:, 0:128], b[0][:, :128], AF.Copy,
                scale=self.ccol(tc, 0, h)),
                reads=[bt2[1], cpr], writes=[rhs0[1]])
            kb.op("DV", lambda e, b=bt1, tc=tc, h=h: e.tensor_scalar(
                rhs0[0][:, 128:256], b[0][:, :128], self.ccol(tc, 1, h),
                -1.0, op0=ALU.mult, op1=ALU.mult),
                reads=[bt1[1], cpr], writes=[rhs0[1]])
            kb.op("DV", lambda e: e.tensor_copy(xb[0][0][:], rhs0[0][:]),
                  reads=[rhs0[1]], writes=[xb[0][1]])
            for it in range(NEU):
                sx, dx = xb[it % 2], xb[(it + 1) % 2]
                bn = self.bank("F")
                kb.op("PE", lambda e, b=bn, s=sx: e.matmul(
                    b[0][:, 0:256], lt[0][:], s[0][:],
                    start=True, stop=True),
                    reads=[lt[1], sx[1]], writes=[bn[1]])
                kb.op("DV", lambda e, b=bn, dx=dx: e.tensor_tensor(
                    dx[0][:], rhs0[0][:], b[0][:, 0:256], op=ALU.subtract),
                    reads=[rhs0[1], bn[1]], writes=[dx[1]])
            xf = xb[NEU % 2]
            bk3 = self.bank("F")
            kb.op("PE", lambda e, b=bk3, x=xf: e.matmul(
                b[0][:, :128], kpr[0][:], x[0][:, 128:256],
                start=True, stop=True),
                reads=[kpr[1], xf[1]], writes=[bk3[1]])
            kb.op("DV", lambda e, tc=tc, h=h: e.tensor_scalar(
                aci[0][:], self.c_idf[0][:], self.ccol(tc, 5, h), None,
                op0=ALU.mult),
                reads=[self.c_idf[1], cpr], writes=[aci[1]])
            kb.op("DV", lambda e, b=bk3: e.tensor_tensor(
                ptile[0][:], aci[0][:], b[0][:, :128], op=ALU.subtract),
                reads=[aci[1], bk3[1]], writes=[ptile[1]])
            bt3 = self.bank("B")
            kb.op("PE", lambda e, b=bt3: e.transpose(
                b[0][:, :128], ptile[0][:], self.c_idb[0][:]),
                reads=[ptile[1], self.c_idb[1]], writes=[bt3[1]])
            kb.op("AC", lambda e, b=bt3: e.copy(pcs[0][:], b[0][:, :128]),
                  reads=[bt3[1]], writes=[pcs[1]])
            bk4 = self.bank("F")
            kb.op("PE", lambda e, b=bk4, x=xf: e.matmul(
                b[0][:, :128], kpr[0][:], x[0][:, 0:128],
                start=True, stop=False),
                reads=[kpr[1], xf[1]], writes=[bk4[1]])
            kb.op("PE", lambda e, b=bk4, h=h: e.matmul(
                b[0][:, :128], pcs[0][:], BTa[h][0][:],
                start=False, stop=True),
                reads=[pcs[1], BTa[h][1]], writes=[bk4[1]])
            bk5 = self.bank("F")
            kb.op("PE", lambda e, b=bk5, h=h: e.matmul(
                b[0][:, :128], pcs[0][:], PTa[h][0][:],
                start=True, stop=True),
                reads=[pcs[1], PTa[h][1]], writes=[bk5[1]])
            kb.op("AC", lambda e, b=bk4, h=h: e.copy(BTa[h][0][:],
                                                     b[0][:, :128]),
                  reads=[bk4[1]], writes=[BTa[h][1]])
            kb.op("AC", lambda e, b=bk5, h=h: e.copy(PTa[h][0][:],
                                                     b[0][:, :128]),
                  reads=[bk5[1]], writes=[PTa[h][1]])

        for hp in range(0, NH, 2):
            load_head(hp, S[0], 0)
            load_head(hp + 1, S[1], 0)
            for tc in range(NCH):
                if tc == 8:
                    load_head(hp, S[0], 1)
                    load_head(hp + 1, S[1], 1)
                chunk1(hp, tc, S[0])
                yield
                chunk1(hp + 1, tc, S[1])
                yield
        cci, ccir = self.cc_in
        pcs = S[0]["pcs"]
        for h in range(NH):
            bt = self.bank("B")
            kb.op("PE", lambda e, b=bt, h=h: e.transpose(
                b[0][:, :128], PTa[h][0][:], self.c_idb[0][:]),
                reads=[PTa[h][1], self.c_idb[1]], writes=[bt[1]])
            kb.op("AC", lambda e, b=bt: e.copy(pcs[0][:], b[0][:, :128]),
                  reads=[bt[1]], writes=[pcs[1]])
            kb.dma("SY", cci[h, 0], pcs[0][:], reads=[pcs[1]], writes=[ccir],
                   ch="sp_cc")
            kb.dma("SY", cci[h, 1], BTa[h][0][:], reads=[BTa[h][1]],
                   writes=[ccir], ch="sp_cc")
        cco, ccor = self.cc_out
        deps = ccir.deps_for_read() + ccor.deps_for_write()
        kb._wait_deps(kb.GP, deps)
        ci = self.nc.gpsimd.collective_compute(
            "AllGather", ALU.bypass,
            replica_groups=[[0, 1, 2, 3], [4, 5, 6, 7]],
            ins=[cci[:]], outs=[cco[:]])
        kb.cc_count += 1
        ci.then_inc(kb.CC, 1)
        kb.barrier()
        st.close()

    def p6_compose(self):
        kb = self.kb
        st = ExitStack()
        pg = self.sbuf("pg", [128, 128], BF16, st)
        bg = self.sbuf("bg", [128, 128], BF16, st)
        t1 = self.sbuf("ct1", [128, 128], F32, st)
        cco, ccor = self.cc_out
        mjt, mjr = self.c_mj
        kb.SY.obj.wait_ge(kb.CC, kb.cc_count)
        for h in range(NH):
            kb.op("DV", lambda e, h=h: e.memset(self.S0T[h][0][:], 0.0),
                  writes=[self.S0T[h][1]])
            for j in range(3):
                kb.dma("SY", pg[0][:], cco[j, h, 0], reads=[ccor],
                       writes=[pg[1]], ch="pg")
                kb.dma("SY", bg[0][:], cco[j, h, 1], reads=[ccor],
                       writes=[bg[1]], ch="bg")
                kb.op("AC", lambda e, h=h: e.copy(self.S0b[h][0][:],
                                                  self.S0T[h][0][:]),
                      reads=[self.S0T[h][1]], writes=[self.S0b[h][1]])
                bk = self.bank("F")
                kb.op("PE", lambda e, b=bk, h=h: e.matmul(
                    b[0][:, :128], pg[0][:], self.S0b[h][0][:],
                    start=True, stop=True),
                    reads=[pg[1], self.S0b[h][1]], writes=[bk[1]])
                kb.op("DV", lambda e, b=bk, j=j: e.tensor_scalar(
                    t1[0][:], b[0][:, :128], mjt[:, j:j + 1], None,
                    op0=ALU.mult),
                    reads=[bk[1], mjr], writes=[t1[1]])
                kb.op("DV", lambda e, h=h, j=j: e.tensor_scalar(
                    self.S0T[h][0][:], self.S0T[h][0][:], mjt[:, 3 + j:4 + j],
                    None, op0=ALU.mult),
                    reads=[self.S0T[h][1], mjr], writes=[self.S0T[h][1]])
                kb.op("DV", lambda e, h=h: e.tensor_tensor(
                    self.S0T[h][0][:], self.S0T[h][0][:], t1[0][:], op=ALU.add),
                    reads=[self.S0T[h][1], t1[1]], writes=[self.S0T[h][1]])
                kb.op("DV", lambda e, j=j: e.tensor_scalar(
                    t1[0][:], bg[0][:], mjt[:, j:j + 1], None, op0=ALU.mult),
                    reads=[bg[1], mjr], writes=[t1[1]])
                kb.op("DV", lambda e, h=h: e.tensor_tensor(
                    self.S0T[h][0][:], self.S0T[h][0][:], t1[0][:], op=ALU.add),
                    reads=[self.S0T[h][1], t1[1]], writes=[self.S0T[h][1]])
            kb.op("AC", lambda e, h=h: e.copy(self.S0b[h][0][:],
                                              self.S0T[h][0][:]),
                  reads=[self.S0T[h][1]], writes=[self.S0b[h][1]])
        kb.barrier()
        st.close()

    def p7_phase2(self):
        kb = self.kb
        st = ExitStack()
        self.p7_stack = st
        rwt = [self.sbuf(f"p2row{i}", [65, 2 * TPC], F32, st)
               for i in range(2)]
        S = []
        for p in range(4):
            d = {}
            for i, nm in enumerate(("kh", "vh", "qh", "gh")):
                d[nm] = self.A2[4 * p + i]
            rt = rwt[p // 2]
            rbase = 64 * (p % 2)
            d["rw"] = ((rt[0], rbase), rt[1])
            for nm, shp, dt in (("kkts", [128, 128], BF16),
                                ("kqts", [128, 128], BF16),
                                ("rc1", [128, 128], F32),
                                ("rc2", [128, 128], F32),
                                ("lt", [128, 128], BF16),
                                ("mqt", [128, 128], BF16),
                                ("kpr", [128, 128], BF16),
                                ("bv", [128, 128], F32),
                                ("rhs", [128, 128], F32),
                                ("ols", [128, 128], F32),
                                ("otok", [128, 128], F32),
                                ("on", [128, 128], F32),
                                ("oc", [128, 8], F32),
                                ("rhst", [128, 128], F32)):
                d[nm] = self.sbuf(f"t_{nm}{p}", shp, dt, st)
            d["xb"] = [self.sbuf(f"t_xb{p}_{i}", [128, 128], BF16, st)
                       for i in range(2)]
            S.append(d)
        cpr = self.colpack[1]
        rd, rdr = self.rows_d

        def load_head(h, d):
            p4_ = h % 4
            kb.dma("SY", d["kh"][0][:], self.kT_s[0][h],
                   reads=[self.kT_s[1]], writes=[d["kh"][1]], ch=f"kh{p4_}")
            kb.dma("SY", d["vh"][0][:], self.vT_s[0][h],
                   reads=[self.vT_s[1]], writes=[d["vh"][1]], ch=f"vh{p4_}")
            kb.dma("SY", d["qh"][0][:], self.qT_s[0][h],
                   reads=[self.qT_s[1]], writes=[d["qh"][1]], ch=f"qh{p4_}")
            kb.dma("SY", d["gh"][0][:], self.gT_s[0][h],
                   reads=[self.gT_s[1]], writes=[d["gh"][1]], ch=f"gh{p4_}")
            (rt, rbase), rres = d["rw"]
            kb.dma("SY", rt[rbase:rbase + 1, 0:TPC], rd[0:1, h, :],
                   reads=[rdr], writes=[rres], ch=f"rw{p4_}")
            kb.dma("SY", rt[rbase:rbase + 1, TPC:2 * TPC], rd[1:2, h, :],
                   reads=[rdr], writes=[rres], ch=f"rw{p4_}")

        def chunk2(h, tc, d, pool=None):
            sl = slice(tc * 128, (tc + 1) * 128)
            kT = (d["kh"][0][:, sl], d["kh"][1])
            vT = (d["vh"][0][:, sl], d["vh"][1])
            qT = (d["qh"][0][:, sl], d["qh"][1])
            gT = (d["gh"][0][:, sl], d["gh"][1])
            (rt, rbase), rres = d["rw"]
            r2row = (rt[rbase:rbase + 1, sl], rres)
            r3row = (rt[rbase:rbase + 1, TPC + tc * 128: TPC + (tc + 1) * 128],
                     rres)
            kkts, kqts = d["kkts"], d["kqts"]
            rc1, rc2 = d["rc1"], d["rc2"]
            e1, e2 = rc1, rc2
            lt, mqt, kpr, bv = d["lt"], d["mqt"], d["kpr"], d["bv"]
            rhs, ols, otok, on = d["rhs"], d["ols"], d["otok"], d["on"]
            oc, rhst, xb = d["oc"], d["rhst"], d["xb"]
            oscr = d["on"]
            ones_sl = self.c_ones[0][rbase:rbase + 1, :]
            b1 = self.bank("F")
            kb.op("PE", lambda e, b=b1, kT=kT: e.matmul(
                b[0][:, 0:128], kT[0], kT[0], start=True, stop=True),
                reads=[kT[1]], writes=[b1[1]])
            kb.op("PE", lambda e, b=b1, kT=kT, qT=qT: e.matmul(
                b[0][:, 128:256], kT[0], qT[0], start=True, stop=True),
                reads=[kT[1], qT[1]], writes=[b1[1]])
            kb.op("DV", lambda e, b=b1: e.tensor_tensor(
                kkts[0][:], b[0][:, 0:128], self.c_msu[0][:], op=ALU.mult),
                reads=[b1[1], self.c_msu[1]], writes=[kkts[1]])
            kb.op("DV", lambda e, b=b1: e.tensor_tensor(
                kqts[0][:], b[0][:, 128:256], self.c_miu[0][:], op=ALU.mult),
                reads=[b1[1], self.c_miu[1]], writes=[kqts[1]])
            yield
            b2 = self.bank("F")
            kb.op("PE", lambda e, b=b2, r=r2row, o=ones_sl: e.matmul(
                b[0][:, 0:128], o, r[0], start=True, stop=True),
                reads=[self.c_ones[1], r2row[1]], writes=[b2[1]])
            kb.op("PE", lambda e, b=b2, r=r3row, o=ones_sl: e.matmul(
                b[0][:, 128:256], o, r[0], start=True, stop=True),
                reads=[self.c_ones[1], r3row[1]], writes=[b2[1]])
            kb.op("DV", lambda e, b=b2, tc=tc, h=h: e.tensor_scalar(
                rc1[0][:], b[0][:, 0:128], self.ccol(tc, 4, h), 0.0,
                op0=ALU.add, op1=ALU.min),
                reads=[b2[1], cpr], writes=[rc1[1]])
            kb.op("DV", lambda e, b=b2, tc=tc, h=h: e.tensor_scalar(
                rc2[0][:], b[0][:, 128:256], self.ccol(tc, 4, h), 0.0,
                op0=ALU.add, op1=ALU.min),
                reads=[b2[1], cpr], writes=[rc2[1]])
            yield
            bt1 = self.bank("B")
            kb.op("PE", lambda e, b=bt1, kT=kT: e.transpose(
                b[0][:, :128], kT[0], self.c_idb[0][:]),
                reads=[kT[1], self.c_idb[1]], writes=[bt1[1]])
            kb.op("AC", lambda e, b=bt1, tc=tc, h=h: e.activation(
                kpr[0][:], b[0][:, :128], AF.Copy,
                scale=self.ccol(tc, 3, h)),
                reads=[bt1[1], cpr], writes=[kpr[1]])
            bt2 = self.bank("B")
            kb.op("PE", lambda e, b=bt2, vT=vT: e.transpose(
                b[0][:, :128], vT[0], self.c_idb[0][:]),
                reads=[vT[1], self.c_idb[1]], writes=[bt2[1]])
            kb.op("AC", lambda e, b=bt2, tc=tc, h=h: e.activation(
                bv[0][:], b[0][:, :128], AF.Copy,
                scale=self.ccol(tc, 0, h)),
                reads=[bt2[1], cpr], writes=[bv[1]])
            yield
            kb.op("AC", lambda e: e.activation(e1[0][:], rc1[0][:], AF.Exp),
                  reads=[rc1[1]], writes=[e1[1]])
            kb.op("AC", lambda e: e.activation(e2[0][:], rc2[0][:], AF.Exp),
                  reads=[rc2[1]], writes=[e2[1]])
            kb.op("DV", lambda e: e.tensor_tensor(lt[0][:], e1[0][:],
                                                  kkts[0][:], op=ALU.mult),
                  reads=[e1[1], kkts[1]], writes=[lt[1]])
            kb.op("DV", lambda e: e.tensor_tensor(mqt[0][:], e2[0][:],
                                                  kqts[0][:], op=ALU.mult),
                  reads=[e2[1], kqts[1]], writes=[mqt[1]])
            yield
            b3 = self.bank("F")
            kb.op("PE", lambda e, b=b3, kT=kT, h=h: e.matmul(
                b[0][:, :128], kT[0], self.S0b[h][0][:],
                start=True, stop=True),
                reads=[kT[1], self.S0b[h][1]], writes=[b3[1]])
            kb.op("AC", lambda e, b=b3, tc=tc, h=h: e.activation(
                rhst[0][:], b[0][:, :128], AF.Copy,
                scale=self.ccol(tc, 1, h)),
                reads=[b3[1], cpr], writes=[rhst[1]])
            kb.op("DV", lambda e: e.tensor_tensor(
                rhs[0][:], rhst[0][:], bv[0][:], op=ALU.add),
                reads=[rhst[1], bv[1]], writes=[rhs[1]])
            kb.op("DV", lambda e: e.tensor_copy(xb[0][0][:], rhs[0][:]),
                  reads=[rhs[1]], writes=[xb[0][1]])
            yield
            for it in range(NEU):
                sx, dx = xb[it % 2], xb[(it + 1) % 2]
                bn = self.bank("F")
                kb.op("PE", lambda e, b=bn, s=sx: e.matmul(
                    b[0][:, 0:128], lt[0][:], s[0][:],
                    start=True, stop=True),
                    reads=[lt[1], sx[1]], writes=[bn[1]])
                kb.op("DV", lambda e, b=bn, dx=dx: e.tensor_tensor(
                    dx[0][:], rhs[0][:], b[0][:, 0:128], op=ALU.subtract),
                    reads=[rhs[1], bn[1]], writes=[dx[1]])
                yield
            U = xb[NEU % 2]
            b4 = self.bank("F")
            kb.op("PE", lambda e, b=b4, U=U: e.matmul(
                b[0][:, 0:128], mqt[0][:], U[0][:], start=True, stop=True),
                reads=[mqt[1], U[1]], writes=[b4[1]])
            kb.op("PE", lambda e, b=b4, qT=qT, h=h: e.matmul(
                b[0][:, 128:256], qT[0], self.S0b[h][0][:],
                start=True, stop=True),
                reads=[qT[1], self.S0b[h][1]], writes=[b4[1]])
            kb.op("AC", lambda e, b=b4: e.copy(ols[0][:], b[0][:, 0:128]),
                  reads=[b4[1]], writes=[ols[1]])
            kb.op("AC", lambda e, b=b4, tc=tc, h=h: e.activation(
                rhst[0][:], b[0][:, 128:256], AF.Copy,
                scale=self.ccol(tc, 2, h)),
                reads=[b4[1], cpr], writes=[rhst[1]])
            kb.op("DV", lambda e: e.tensor_tensor(
                otok[0][:], rhst[0][:], ols[0][:], op=ALU.add),
                reads=[rhst[1], ols[1]], writes=[otok[1]])
            yield
            b5 = self.bank("F")
            kb.op("PE", lambda e, b=b5, U=U: e.matmul(
                b[0][:, :128], kpr[0][:], U[0][:], start=True, stop=True),
                reads=[kpr[1], U[1]], writes=[b5[1]])
            kb.op("DV", lambda e, tc=tc, h=h: e.tensor_scalar(
                self.S0T[h][0][:], self.S0T[h][0][:], self.ccol(tc, 5, h),
                None, op0=ALU.mult),
                reads=[self.S0T[h][1], cpr], writes=[self.S0T[h][1]])
            kb.op("DV", lambda e, b=b5, h=h: e.tensor_tensor(
                self.S0T[h][0][:], self.S0T[h][0][:], b[0][:, :128],
                op=ALU.add),
                reads=[self.S0T[h][1], b5[1]], writes=[self.S0T[h][1]])
            kb.op("AC", lambda e, h=h: e.copy(self.S0b[h][0][:],
                                              self.S0T[h][0][:]),
                  reads=[self.S0T[h][1]], writes=[self.S0b[h][1]])
            yield
            kb.op("DV", lambda e: e.tensor_reduce(oc[0][:, 0:1], otok[0][:],
                                                  axis=AX.X, op=ALU.add),
                  reads=[otok[1]], writes=[oc[1]])
            kb.op("AC", lambda e: e.activation(
                oscr[0][:], otok[0][:], AF.Square,
                accum_out=oc[0][:, 1:2]),
                reads=[otok[1]], writes=[oscr[1], oc[1]])
            yield
            kb.op("DV", lambda e: e.tensor_scalar(oc[0][:, 2:3],
                                                  oc[0][:, 0:1], 1.0 / D,
                                                  None, op0=ALU.mult),
                  reads=[oc[1]], writes=[oc[1]])
            kb.op("DV", lambda e: e.tensor_tensor(oc[0][:, 3:4],
                                                  oc[0][:, 2:3],
                                                  oc[0][:, 2:3],
                                                  op=ALU.mult),
                  reads=[oc[1]], writes=[oc[1]])
            kb.op("DV", lambda e: e.tensor_scalar(oc[0][:, 4:5],
                                                  oc[0][:, 1:2], 1.0 / D,
                                                  oc[0][:, 3:4],
                                                  op0=ALU.mult,
                                                  op1=ALU.subtract),
                  reads=[oc[1]], writes=[oc[1]])
            kb.op("AC", lambda e: e.activation(oc[0][:, 5:6], oc[0][:, 4:5],
                                               AF.Ln,
                                               bias=self.c_eps[0][:, 0:1]),
                  reads=[oc[1], self.c_eps[1]], writes=[oc[1]])
            kb.op("AC", lambda e: e.activation(oc[0][:, 6:7], oc[0][:, 5:6],
                                               AF.Exp, scale=-0.5),
                  reads=[oc[1]], writes=[oc[1]])
            yield
            kb.op("DV", lambda e: e.tensor_scalar(on[0][:], otok[0][:],
                                                  oc[0][:, 2:3],
                                                  oc[0][:, 6:7],
                                                  op0=ALU.subtract,
                                                  op1=ALU.mult),
                  reads=[otok[1], oc[1]], writes=[on[1]])
            b6 = self.bank("F")
            kb.op("PE", lambda e, b=b6: e.transpose(
                b[0][:, :128], on[0][:], self.c_idf[0][:]),
                reads=[on[1], self.c_idf[1]], writes=[b6[1]])
            kb.op("DV", lambda e, b=b6: e.tensor_scalar(
                rhst[0][:], b[0][:, :128], self.c_lnw[0][:, :],
                self.c_lnb[0][:, :], op0=ALU.mult, op1=ALU.add),
                reads=[b6[1], self.c_lnw[1], self.c_lnb[1]],
                writes=[rhst[1]])
            kb.op("DV", lambda e, g=gT, tc=tc, h=h: e.tensor_tensor(
                self.A1[tc][0][:, h * 128:(h + 1) * 128], rhst[0][:],
                g[0], op=ALU.mult),
                reads=[rhst[1], gT[1]], writes=[self.A1[tc][1]])

        for hp in range(0, NH, 4):
            for q in range(4):
                load_head(hp + q, S[q])
            for tc in range(NCH):
                alive = [chunk2(hp + q, tc, S[q]) for q in range(4)]
                while alive:
                    nxt = []
                    for g in alive:
                        try:
                            next(g)
                            nxt.append(g)
                        except StopIteration:
                            pass
                    alive = nxt
        kb.barrier()
        st.close()

    def p7b_epilogue(self):
        kb = self.kb
        st = ExitStack()
        wo, wor = self.Wo
        for i in range(16):
            tokw = kb.dma("GP", self.A2[i][0][:], wo[i * 128:(i + 1) * 128, :],
                          reads=[wor], writes=[self.A2[i][1]], ch="wgt")
        for i in range(16):
            self.A2[i][1].prod = tokw
        xbf = [self.sbuf(f"exb{i}", [128, H], F32, st) for i in range(2)]
        x1c = self.sbuf("x1c", [128, H], F32, st)
        xn1c = self.sbuf("xn1c", [128, H], BF16, st)
        lnscr = self.sbuf("lnscr2", [128, H], F32, st)
        xr, xres = self.x_r
        x1t, x1r = self.x1_s
        xn1t, xn1r = self.xn1_s
        for tc in range(NCH):
            bx = xbf[tc % 2]
            kb.dma("SY", bx[0][:], xr[128 + tc * 128: 128 + (tc + 1) * 128, :],
                   reads=[xres], writes=[bx[1]], ch=f"x{tc % 2}")
            for dmg in range(4):
                bk = self.bank("F")
                for f in range(16):
                    kb.op("PE", lambda e, b=bk, f=f, tc=tc, dmg=dmg: e.matmul(
                        b[0][:, :], self.A1[tc][0][:, f * 128:(f + 1) * 128],
                        self.A2[f][0][:, dmg * 512:(dmg + 1) * 512],
                        start=(f == 0), stop=(f == 15)),
                        reads=[self.A1[tc][1], self.A2[f][1]],
                        writes=([bk[1]] if f in (0, 15) else []))
                kb.op("DV", lambda e, b=bk, bx=bx, dmg=dmg: e.tensor_tensor(
                    x1c[0][:, dmg * 512:(dmg + 1) * 512], b[0][:, :],
                    bx[0][:, dmg * 512:(dmg + 1) * 512], op=ALU.add),
                    reads=[bk[1], bx[1]], writes=[x1c[1]])
            self.emit_ln(x1c, xn1c, H, scr=lnscr)
            kb.dma("SY", x1t[tc], x1c[0][:], reads=[x1c[1]], writes=[x1r],
                   ch="sp_x1")
            kb.dma("SY", xn1t[tc], xn1c[0][:], reads=[xn1c[1]],
                   writes=[xn1r], ch="sp_xn1")
        kb.barrier()
        st.close()

    def p8_ffn(self):
        kb = self.kb
        st = ExitStack()
        xn1t, xn1r = self.xn1_s
        xbuf = [self.sbuf(f"fxb{i}", [128, H], BF16, st) for i in range(2)]
        for tc in range(NCH):
            bx = xbuf[tc % 2]
            kb.dma("SY", bx[0][:], xn1t[tc], reads=[xn1r], writes=[bx[1]],
                   ch=f"x{tc % 2}")
            for d in range(16):
                bt = self.bank("B")
                kb.op("PE", lambda e, b=bt, bx=bx, d=d: e.transpose(
                    b[0][:, :128], bx[0][:, d * 128:(d + 1) * 128],
                    self.c_idb[0][:]),
                    reads=[bx[1], self.c_idb[1]], writes=[bt[1]])
                kb.op("AC", lambda e, b=bt, d=d, tc=tc: e.copy(
                    self.A2[d][0][:, tc * 128:(tc + 1) * 128], b[0][:, :128]),
                    reads=[bt[1]], writes=[self.A2[d][1]])
        kb.barrier()
        self.psumB_stack.close()
        self.banksF.append(self.psum("pf6", [128, 512], F32, st))
        self.banksF.append(self.psum("pf7", [128, 512], F32, st))
        gate = [self.sbuf(f"gt{i}", [128, 512], BF16, st) for i in range(4)]
        fuse_st = [self.sbuf(f"fst{i}", [128, H], BF16, st) for i in range(2)]
        outb = [self.sbuf(f"ob{i}", [128, 1024], F32, st) for i in range(4)]
        x1p = [self.sbuf(f"fx1_{i}", [128, 1024], F32, st) for i in range(4)]
        oTt = [self.sbuf(f"oT{i}", [128, 512], F32, st) for i in range(4)]
        ftmp = self.sbuf("ftmp", [128, 512], F32, st)
        fuse_d = self.dram("fuse_d", [16, 128, H], BF16)
        wg8, wg8r = self.wg8
        wu8, wu8r = self.wu8
        wd8, wd8r = self.wd8
        x1t, x1r = self.x1_s
        outt, outres = self.out_r

        def wslot(b, kind, c):
            return self.A1[8 * b + 4 * kind + c // 4], (c % 4) * 512

        def load_wblk(tg, blk, b):
            tok = None
            for kind, (w8, w8r) in ((0, (wg8, wg8r)), (1, (wu8, wu8r))):
                for q in range(4):
                    t = self.A1[8 * b + 4 * kind + q]
                    tok = kb.dma("GP", t[0][:, 0:2048],
                                 w8[blk, :, q * 2048:(q + 1) * 2048],
                                 reads=[w8r], writes=[t[1]], ch=f"w8{b}")
            for kind in range(2):
                for q in range(4):
                    self.A1[8 * b + 4 * kind + q][1].prod = tok

        for tg in range(4):
            ts_ = slice(tg * 512, (tg + 1) * 512)
            if tg == 0:
                load_wblk(tg, 0, 0)
            for blk in range(16):
                b = blk % 2
                if blk < 15:
                    load_wblk(tg, blk + 1, 1 - b)
                for ic in range(4):
                    bk = self.banksF[ic]
                    for c in range(16):
                        t, off = wslot(b, 0, c)
                        kb.op("PE", lambda e, bk=bk, t=t, off=off, ic=ic,
                              ts_=ts_, c=c: e.matmul(
                            bk[0][:, :], t[0][:, off + ic * 128:off + (ic + 1) * 128],
                            self.A2[c][0][:, ts_], start=(c == 0),
                            stop=(c == 15)),
                            reads=[t[1], self.A2[c][1]],
                            writes=([bk[1]] if c in (0, 15) else []))
                    kb.op("AC", lambda e, bk=bk, ic=ic, blk=blk: e.activation(
                        gate[ic][0][:], bk[0][:, :], AF.Silu,
                        scale=self.c_sg[0][:, blk * 4 + ic:blk * 4 + ic + 1]),
                        reads=[bk[1], self.c_sg[1]], writes=[gate[ic][1]])
                fst = fuse_st[b]
                for ic in range(4):
                    bk = self.banksF[4 + ic % 4] if ic < 4 else None
                    bk = self.banksF[(4 + ic) % 8]
                    for c in range(16):
                        t, off = wslot(b, 1, c)
                        kb.op("PE", lambda e, bk=bk, t=t, off=off, ic=ic,
                              ts_=ts_, c=c: e.matmul(
                            bk[0][:, :], t[0][:, off + ic * 128:off + (ic + 1) * 128],
                            self.A2[c][0][:, ts_], start=(c == 0),
                            stop=(c == 15)),
                            reads=[t[1], self.A2[c][1]],
                            writes=([bk[1]] if c in (0, 15) else []))
                    kb.op("AC", lambda e, bk=bk, ic=ic, blk=blk: e.activation(
                        ftmp[0][:], bk[0][:, :], AF.Copy,
                        scale=self.c_su[0][:, blk * 4 + ic:blk * 4 + ic + 1]),
                        reads=[bk[1], self.c_su[1]], writes=[ftmp[1]])
                    kb.op("DV", lambda e, ic=ic, fst=fst: e.tensor_tensor(
                        fst[0][:, ic * 512:(ic + 1) * 512], ftmp[0][:],
                        gate[ic][0][:], op=ALU.mult),
                        reads=[ftmp[1], gate[ic][1]], writes=[fst[1]])
                kb.dma("SY", fuse_d[0][blk], fst[0][:], reads=[fst[1]],
                       writes=[fuse_d[1]], ch=f"sp_f{b}")
            for bb in range(2):
                c = kb.chan(f"sp_f{bb}")
                kb.SY.wait_tok((c, c.count))
            fuse_d[1].prod = None
            for dmh in range(2):
                for grp in range(8):
                    gp = grp % 2
                    for q in range(4):
                        t = self.A1[8 + gp * 4 + q]
                        kb.dma("GP", t[0][:, 0:2048],
                               wd8[dmh, grp, :, q * 2048:(q + 1) * 2048],
                               reads=[wd8r], writes=[t[1]],
                               ch=f"wd{gp * 4 + q}")
                    for q2 in range(2):
                        t = self.A1[2 * gp + q2]
                        kb.dma("SY", t[0][:, 0:2048],
                               fuse_d[0][grp * 2 + q2],
                               reads=[fuse_d[1]], writes=[t[1]],
                               ch=f"fl{gp * 2 + q2}")
                    for ic8 in range(8):
                        icg = grp * 8 + ic8
                        wt = self.A1[8 + gp * 4 + ic8 // 2]
                        woff = (ic8 % 2) * 1024
                        ft = self.A1[2 * gp + ic8 // 4]
                        foff = (ic8 % 4) * 512
                        for dmi in range(8):
                            bk = self.banksF[dmi]
                            kb.op("PE", lambda e, bk=bk, wt=wt, woff=woff,
                                  ft=ft, foff=foff, dmi=dmi, icg=icg:
                                  e.matmul(
                                bk[0][:, :],
                                wt[0][:, woff + dmi * 128:woff + (dmi + 1) * 128],
                                ft[0][:, foff:foff + 512],
                                start=(icg == 0), stop=(icg == 63)),
                                reads=[wt[1], ft[1]],
                                writes=([bk[1]] if icg in (0, 63) else []))
                for tt4 in range(4):
                    tcg = tg * 4 + tt4
                    kb.dma("SY", x1p[tt4][0][:],
                           x1t[tcg, :, dmh * 1024:(dmh + 1) * 1024],
                           reads=[x1r], writes=[x1p[tt4][1]], ch=f"x1p{tt4}")
                for half in range(2):
                    for di in range(4):
                        dmi = half * 4 + di
                        bk = self.banksF[dmi]
                        kb.op("AC", lambda e, bk=bk, ot=oTt[di], dmh=dmh,
                              dmi=dmi: e.activation(
                            ot[0][:], bk[0][:, :], AF.Copy,
                            scale=self.c_sd[0][:, dmh * 8 + dmi:
                                               dmh * 8 + dmi + 1]),
                            reads=[bk[1], self.c_sd[1]], writes=[oTt[di][1]])
                    for di in range(4):
                        dmi = half * 4 + di
                        ot = oTt[di]
                        for tt4 in range(4):
                            bkt = self.banksF[half * 4 + (di * 4 + tt4) % 2]
                            kb.op("PE", lambda e, b=bkt, ot=ot, tt4=tt4:
                                  e.transpose(
                                b[0][:, :128],
                                ot[0][:, tt4 * 128:(tt4 + 1) * 128],
                                self.c_idf[0][:]),
                                reads=[ot[1], self.c_idf[1]], writes=[bkt[1]])
                            kb.op("DV", lambda e, b=bkt, tt4=tt4, dmi=dmi:
                                  e.tensor_tensor(
                                outb[tt4][0][:, dmi * 128:(dmi + 1) * 128],
                                b[0][:, :128],
                                x1p[tt4][0][:, dmi * 128:(dmi + 1) * 128],
                                op=ALU.add),
                                reads=[bkt[1], x1p[tt4][1]],
                                writes=[outb[tt4][1]])
                for tt4 in range(4):
                    tcg = tg * 4 + tt4
                    kb.dma("SY",
                           outt[tcg * 128:(tcg + 1) * 128,
                                dmh * 1024:(dmh + 1) * 1024],
                           outb[tt4][0][:, :],
                           reads=[outb[tt4][1]], writes=[outres],
                           ch=f"ob{tt4}")
            if tg < 3:
                load_wblk(tg + 1, 0, 0)
        kb.barrier()
        st.close()

    def finish(self):
        kb = self.kb
        for name in self.taps:
            t, r = getattr(self, name)
            ot, orr = self.dram(f"tap_{name}", list(t.shape), t.dtype,
                                "ExternalOutput")
            kb.dma("SY", ot[:], t[:], reads=[r], writes=[orr], ch="tap")
        for c in kb.chans.values():
            if c.count:
                kb.SY.obj.wait_ge(c.sem, c.count)
        if kb.cc_count:
            kb.SY.obj.wait_ge(kb.CC, kb.cc_count)
        for e in (kb.PE, kb.DV, kb.AC):
            t = e.token()
            if t is not None:
                kb.SY.obj.wait_ge(e.sem, t[1])
        return self.nc


def kernel(**inp):
    out, _ = _run(inp, TAPS, STOP_AFTER)
    return out


def _run(inp, taps, stop_after):
    x = np.asarray(inp["L0_attn_x"])
    ident = np.eye(128, dtype=np.float32)
    shared = dict(
        Wq=np.ascontiguousarray(inp["L0_attn_Wq"], np.float32),
        Wk=np.ascontiguousarray(inp["L0_attn_Wk"], np.float32),
        Wv=np.ascontiguousarray(inp["L0_attn_Wv"], np.float32),
        Wg=np.ascontiguousarray(inp["L0_attn_Wg"], np.float32),
        Wo=np.ascontiguousarray(inp["L0_attn_Wo"], np.float32),
        WaT=np.ascontiguousarray(inp["L0_attn_Wa"].T, np.float32),
        WbT=np.ascontiguousarray(inp["L0_attn_Wb"].T, np.float32),
        ba_r=np.ascontiguousarray(inp["L0_attn_ba"].reshape(NH, 1), np.float32),
        bb_r=np.ascontiguousarray(inp["L0_attn_bb"].reshape(NH, 1), np.float32),
        qcw_r=np.ascontiguousarray(np.asarray(inp["L0_attn_qcw"])
                                   .reshape(16, 128, 4).transpose(1, 0, 2)
                                   .reshape(128, 64), np.float32),
        kcw_r=np.ascontiguousarray(np.asarray(inp["L0_attn_kcw"])
                                   .reshape(16, 128, 4).transpose(1, 0, 2)
                                   .reshape(128, 64), np.float32),
        vcw_r=np.ascontiguousarray(np.asarray(inp["L0_attn_vcw"])
                                   .reshape(16, 128, 4).transpose(1, 0, 2)
                                   .reshape(128, 64), np.float32),
        qcb_r=np.ascontiguousarray(np.asarray(inp["L0_attn_qcb"])
                                   .reshape(16, 128).T, np.float32),
        kcb_r=np.ascontiguousarray(np.asarray(inp["L0_attn_kcb"])
                                   .reshape(16, 128).T, np.float32),
        vcb_r=np.ascontiguousarray(np.asarray(inp["L0_attn_vcb"])
                                   .reshape(16, 128).T, np.float32),
        lnw_r=np.ascontiguousarray(inp["L0_attn_ln_w"].reshape(128, 1),
                                   np.float32),
        lnb_r=np.ascontiguousarray(inp["L0_attn_ln_b"].reshape(128, 1),
                                   np.float32),
        wg8=np.ascontiguousarray(np.asarray(inp["L0_ffn_w_gate"])
                                 .reshape(16, 128, 16, 512)
                                 .transpose(2, 1, 0, 3).reshape(16, 128, INTER)),
        wu8=np.ascontiguousarray(np.asarray(inp["L0_ffn_w_up"])
                                 .reshape(16, 128, 16, 512)
                                 .transpose(2, 1, 0, 3).reshape(16, 128, INTER)),
        wd8=np.ascontiguousarray(np.asarray(inp["L0_ffn_w_down"])
                                 .reshape(8, 8, 128, 2, 1024)
                                 .transpose(3, 0, 2, 1, 4)
                                 .reshape(2, 8, 128, 8192)),
        sg_r=np.ascontiguousarray(np.asarray(inp["L0_ffn_s_gate"])
                                  .reshape(64, 128).T, np.float32),
        su_r=np.ascontiguousarray(np.asarray(inp["L0_ffn_s_up"])
                                  .reshape(64, 128).T, np.float32),
        sd_r=np.ascontiguousarray(np.asarray(inp["L0_ffn_s_down"])
                                  .reshape(16, 128).T, np.float32),
        idf=ident,
        idb=ident.astype(bfdt),
        msu=np.ascontiguousarray(np.triu(np.ones((128, 128), np.float32), 1)),
        miu=np.ascontiguousarray(np.triu(np.ones((128, 128), np.float32), 0)),
        onesr=np.ones((65, 128), np.float32),
    )
    in_maps = []
    for c in range(8):
        bi, g = c // 4, c % 4
        rs = g * TPC
        halo = (x[bi, rs - 3: rs] if g > 0 else np.zeros((3, H), np.float32))
        x_r = np.ascontiguousarray(
            np.concatenate([np.zeros((125, H), np.float32), halo,
                            x[bi, rs: rs + TPC]], axis=0), np.float32)
        mj = np.zeros((128, 6), np.float32)
        for j in range(3):
            mj[:, j] = 1.0 if j < g else 0.0
            mj[:, 3 + j] = 0.0 if j < g else 1.0
        in_maps.append(dict(shared, x_r=x_r, mj_r=mj))
    b = Build(taps=taps, stop_after=stop_after)
    nc = b.build()
    res = run_bass_kernel_spmd(nc, in_maps, core_ids=list(range(8)))
    out = np.empty((B, S, H), np.float32)
    for c in range(8):
        bi, g = c // 4, c % 4
        out[bi, g * TPC:(g + 1) * TPC] = res.results[c]["out_r"]
    return out, res.results


# version 10
# speedup vs baseline: 1.0944x; 1.0944x over previous
import numpy as np
import ml_dtypes
from contextlib import ExitStack

import concourse.bass as bass
import concourse.mybir as mybir
from concourse.bass_utils import run_bass_kernel_spmd

F32 = mybir.dt.float32
BF16 = mybir.dt.bfloat16
I8 = mybir.dt.int8
AF = mybir.ActivationFunctionType
ALU = mybir.AluOpType
AX = mybir.AxisListType
bfdt = ml_dtypes.bfloat16

B, S, H = 2, 8192, 2048
NH, D = 16, 128
TPC = 2048
NCH = TPC // 128
NEU = 2
INTER = 8192
DSCALE = float(D) ** -0.5
LN_EPS = 1e-5

TAPS = ()
PROJ_BARRIER = 0
STOP_AFTER = None


class _Eng:
    def __init__(self, nc, obj, name, sem, is_dma):
        self.nc, self.obj, self.name = nc, obj, name
        self.sem, self.is_dma = sem, is_dma
        self.count = 0
        self.last = None
        self.last_inced = True
        self.hw = {}

    def token(self):
        if self.last is None:
            return None
        if not self.last_inced:
            self.last.then_inc(self.sem, 1)
            self.count += 1
            self.last_inced = True
        return (self, self.count)

    def wait_tok(self, tok):
        if tok is None:
            return
        eng, val = tok
        if val <= 0 or eng is self:
            return
        if self.hw.get(eng.name, 0) >= val:
            return
        self.obj.wait_ge(eng.sem, val)
        self.hw[eng.name] = val


class Res:
    __slots__ = ("name", "prod", "readers")
    def __init__(self, name):
        self.name = name
        self.prod = None
        self.readers = {}

    def deps_for_read(self):
        return [self.prod]

    def deps_for_write(self):
        return [self.prod] + list(self.readers.values())


class Chan:
    __slots__ = ("name", "sem", "count")
    def __init__(self, name, sem):
        self.name, self.sem, self.count = name, sem, 0


class KB:
    def __init__(self, nc, ctx):
        self.nc = nc
        self.ctx = ctx
        sem = lambda n: ctx.enter_context(nc.semaphore(n))
        self.PE = _Eng(nc, nc.tensor, "PE", sem("sPE"), False)
        self.DV = _Eng(nc, nc.vector, "DV", sem("sDV"), False)
        self.AC = _Eng(nc, nc.scalar, "AC", sem("sAC"), False)
        self.SY = _Eng(nc, nc.sync, "SY", sem("sSY"), True)
        self.GP = _Eng(nc, nc.gpsimd, "GP", sem("sGP"), False)
        self.CC = sem("sCC")
        self.cc_count = 0
        self.engs = dict(PE=self.PE, DV=self.DV, AC=self.AC, GP=self.GP)
        self.chans = {}

    def chan(self, name):
        if name not in self.chans:
            self.chans[name] = Chan(
                name, self.ctx.enter_context(self.nc.semaphore("ch_" + name)))
        return self.chans[name]

    def _wait_deps(self, eng, deps):
        for d in deps:
            if d is None:
                continue
            if isinstance(d, tuple) and d[0] == "m":
                src = d[1]
                if src is eng and eng.name == "PE":
                    continue
                d = src.token()
            src = d[0]
            if src is eng:
                if eng.name == "PE":
                    continue
                if eng.hw.get(eng.name, 0) >= d[1]:
                    continue
                eng.obj.wait_ge(eng.sem, d[1])
                eng.hw[eng.name] = d[1]
                continue
            eng.wait_tok(d)

    def _freeze(self, res_list):
        for r in res_list:
            p = r.prod
            if isinstance(p, tuple) and p[0] == "m":
                r.prod = p[1].token()
            for k, v in list(r.readers.items()):
                if isinstance(v, tuple) and v[0] == "m":
                    r.readers[k] = v[1].token()

    def op(self, eng_name, fn, reads=(), writes=()):
        eng = self.engs[eng_name]
        self._freeze(reads)
        self._freeze(writes)
        deps = []
        for r in reads:
            deps.extend(r.deps_for_read())
        for w in writes:
            deps.extend(w.deps_for_write())
        self._wait_deps(eng, deps)
        instr = fn(eng.obj)
        eng.last = instr
        eng.last_inced = False
        for w in writes:
            w.prod = ("m", eng)
            w.readers = {}
        for r in reads:
            r.readers[eng.name] = ("m", eng)
        return instr

    def dma(self, q_name, dst_ap, src_ap, reads=(), writes=(), ch=None):
        eng = self.SY if q_name == "SY" else self.GP
        c = self.chan(ch if ch is not None else ("d" + q_name))
        self._freeze(reads)
        self._freeze(writes)
        deps = []
        for r in reads:
            deps.extend(r.deps_for_read())
        for w in writes:
            deps.extend(w.deps_for_write())
        self._wait_deps(eng, deps)
        instr = eng.obj.dma_start(dst_ap, src_ap)
        c.count += 16
        instr.then_inc(c.sem, 16)
        tok = (c, c.count)
        for w in writes:
            w.prod = tok
            w.readers = {}
        for r in reads:
            r.readers[c.name] = tok
        return tok

    def dma_t(self, q_name, dst_ap, src_ap, reads=(), writes=(), ch=None):
        eng = self.SY if q_name == "SY" else self.AC
        c = self.chan(ch if ch is not None else ("t" + q_name))
        self._freeze(reads)
        self._freeze(writes)
        deps = []
        for r in reads:
            deps.extend(r.deps_for_read())
        for w in writes:
            deps.extend(w.deps_for_write())
        self._wait_deps(eng, deps)
        instr = eng.obj.dma_start_transpose(dst_ap, src_ap)
        c.count += 16
        instr.then_inc(c.sem, 16)
        tok = (c, c.count)
        for w in writes:
            w.prod = tok
            w.readers = {}
        for r in reads:
            r.readers[c.name] = tok
        return tok

    def barrier(self):
        toks = [self.PE.token(), self.DV.token(), self.AC.token()]
        toks += [(c, c.count) for c in self.chans.values() if c.count]
        for e in (self.PE, self.DV, self.AC, self.SY, self.GP):
            for t in toks:
                if t is not None:
                    e.wait_tok(t)


class Build:
    def __init__(self, taps=(), stop_after=None):
        self.taps = tuple(taps)
        self.stop_after = stop_after
        self.nc = bass.Bass("TRN2", target_bir_lowering=False)
        self.base = ExitStack()
        self.kb = KB(self.nc, self.base)

    def sbuf(self, name, shape, dt, stack=None):
        t = (stack or self.base).enter_context(self.nc.sbuf_tensor(name, shape, dt))
        return (t, Res(name))

    def psum(self, name, shape, dt, stack=None):
        t = (stack or self.base).enter_context(self.nc.psum_tensor(name, shape, dt))
        return (t, Res(name))

    def dram(self, name, shape, dt, kind=None, addr_space=None):
        kw = {}
        if kind:
            kw["kind"] = kind
        if addr_space:
            kw["addr_space"] = addr_space
        t = self.nc.dram_tensor(name, shape, dt, **kw)
        return (t, Res(name))

    def bank(self, kind="F"):
        pool = self.banksF if kind == "F" else self.banksB
        i = self._rot[kind] % len(pool)
        self._rot[kind] += 1
        return pool[i]

    def build(self):
        kb = self.kb
        dr = self.dram
        self.x_r = dr("x_r", [128 + TPC, H], F32, "ExternalInput")
        self.out_r = dr("out_r", [TPC, H], F32, "ExternalOutput")
        for w in ("Wq", "Wk", "Wv", "Wg", "Wo"):
            setattr(self, w, dr(w, [H, H], F32, "ExternalInput"))
        self.WaT = dr("WaT", [NH, H], F32, "ExternalInput")
        self.WbT = dr("WbT", [NH, H], F32, "ExternalInput")
        self.ba_r = dr("ba_r", [NH, 1], F32, "ExternalInput")
        self.bb_r = dr("bb_r", [NH, 1], F32, "ExternalInput")
        for w in ("qcw_r", "kcw_r", "vcw_r"):
            setattr(self, w, dr(w, [128, 64], F32, "ExternalInput"))
        for w in ("qcb_r", "kcb_r", "vcb_r"):
            setattr(self, w, dr(w, [128, 16], F32, "ExternalInput"))
        self.lnw_r = dr("lnw_r", [128, 1], F32, "ExternalInput")
        self.lnb_r = dr("lnb_r", [128, 1], F32, "ExternalInput")
        self.wg8 = dr("wg8", [16, 128, INTER], I8, "ExternalInput")
        self.wu8 = dr("wu8", [16, 128, INTER], I8, "ExternalInput")
        self.wd8 = dr("wd8", [2, 8, 128, 8192], I8, "ExternalInput")
        self.sg_r = dr("sg_r", [128, 64], F32, "ExternalInput")
        self.su_r = dr("su_r", [128, 64], F32, "ExternalInput")
        self.sd_r = dr("sd_r", [128, 16], F32, "ExternalInput")
        self.idf = dr("idf", [128, 128], F32, "ExternalInput")
        self.idb = dr("idb", [128, 128], BF16, "ExternalInput")
        self.msu = dr("msu", [128, 128], F32, "ExternalInput")
        self.miu = dr("miu", [128, 128], F32, "ExternalInput")
        self.onesr = dr("onesr", [65, 128], F32, "ExternalInput")
        self.mj_r = dr("mj_r", [128, 6], F32, "ExternalInput")
        self.kT_s = dr("kT_s", [NH, 128, TPC], BF16)
        self.vT_s = dr("vT_s", [NH, 128, TPC], BF16)
        self.qT_s = dr("qT_s", [NH, 128, TPC], BF16)
        self.gT_s = dr("gT_s", [NH, 128, TPC], BF16)
        self.rows_d = dr("rows_d", [2, NH, TPC], F32)
        self.x1_s = dr("x1_s", [NCH, 128, H], F32)
        self.xn1_s = dr("xn1_s", [NCH, 128, H], BF16)
        self.cc_in = dr("cc_in", [NH, 2, 128, 128], BF16)
        self.cc_out = dr("cc_out", [4, NH, 2, 128, 128], BF16)

        self.A1 = [self.sbuf(f"A1_{i}", [128, TPC], BF16) for i in range(16)]
        self.A2 = [self.sbuf(f"A2_{i}", [128, TPC], BF16) for i in range(16)]
        self.banksF = [self.psum(f"pf{i}", [128, 512], F32) for i in range(6)]
        self.psumB_stack = ExitStack()
        self.banksB = [self.psum(f"pb{i}", [128, 1024], BF16,
                                 stack=self.psumB_stack) for i in range(2)]
        self._rot = {"F": 0, "B": 0}
        self.c_idf = self.sbuf("c_idf", [128, 128], F32)
        self.c_idb = self.sbuf("c_idb", [128, 128], BF16)
        self.c_msu = self.sbuf("c_msu", [128, 128], F32)
        self.c_miu = self.sbuf("c_miu", [128, 128], F32)
        self.c_ones = self.sbuf("c_ones", [65, 128], F32)
        self.c_mj = self.sbuf("c_mj", [128, 6], F32)
        self.c_lnw = self.sbuf("c_lnw", [128, 1], F32)
        self.c_lnb = self.sbuf("c_lnb", [128, 1], F32)
        self.c_ba = self.sbuf("c_ba", [NH, 1], F32)
        self.c_bb = self.sbuf("c_bb", [NH, 1], F32)
        self.c_sg = self.sbuf("c_sg", [128, 64], F32)
        self.c_su = self.sbuf("c_su", [128, 64], F32)
        self.c_sd = self.sbuf("c_sd", [128, 16], F32)
        self.c_eps = self.sbuf("c_eps", [128, 1], F32)
        self.lncols = self.sbuf("lncols", [128, 8], F32)
        self.xnT_halo = self.sbuf("xnT_halo", [128, 48], BF16)

        self.scan_stack = ExitStack()
        ss = self.scan_stack
        self.c_cvw = {k: self.sbuf(f"c_{k}w", [128, 64], F32, ss)
                      for k in ("q", "k", "v")}
        self.c_cvb = {k: self.sbuf(f"c_{k}b", [128, 16], F32, ss)
                      for k in ("q", "k", "v")}
        self.c_ones16 = self.sbuf("c_ones16", [NH, 128], F32, ss)
        self.colpack = self.sbuf("colpack", [128, NCH * 6 * NH], F32, ss)
        self.S0T = [self.sbuf(f"S0T_{h}", [128, 128], F32, ss) for h in range(NH)]
        self.S0b = [self.sbuf(f"S0b_{h}", [128, 128], BF16, ss) for h in range(NH)]

        ns = self.nc.named_scope
        with ns("p0"):
            self.p0_consts()
        if self.stop_after == "consts":
            return self.finish()
        with ns("p1_xn"):
            self.p1_xn()
        if self.stop_after == "xn":
            return self.finish()
        if self.stop_after == "xnT_dump":
            for d in range(16):
                self.kb.dma("SY", self.qT_s[0][d], self.A1[d][0][:],
                            reads=[self.A1[d][1]], writes=[self.qT_s[1]],
                            ch="tap")
            self.kb.dma("SY", self.gT_s[0][0][:, 0:48],
                        self.xnT_halo[0][:],
                        reads=[self.xnT_halo[1]], writes=[self.gT_s[1]],
                        ch="tap")
            return self.finish()
        with ns("p3_ab"):
            self.p3_ab()
        with ns("projK"):
            self.proj("Wk", "k", self.kT_s, silu=True, scale=DSCALE)
        with ns("projV"):
            self.proj("Wv", "v", self.vT_s, silu=True, scale=1.0)
        if self.stop_after == "proj_kv":
            return self.finish()
        with ns("p4_phase1"):
            self.p4_phase1()
        if self.stop_after == "phase1":
            return self.finish()
        with ns("projQ"):
            self.proj("Wq", "q", self.qT_s, silu=True, scale=1.0)
        with ns("projG"):
            self.proj("Wg", None, self.gT_s, silu=False, scale=1.0)
        if self.stop_after == "proj_qg":
            return self.finish()
        with ns("p6_compose"):
            self.p6_compose()
        if self.stop_after == "compose":
            return self.finish()
        with ns("p7_phase2"):
            self.p7_phase2()
        if self.stop_after == "phase2":
            return self.finish()
        with ns("p7b_epi"):
            self.p7b_epilogue()
        if self.stop_after == "epilogue":
            return self.finish()
        self.kb.barrier()
        self.scan_stack.close()
        with ns("p8_ffn"):
            self.p8_ffn()
        return self.finish()

    def p0_consts(self):
        kb = self.kb
        loads = [
            (self.c_idf, self.idf), (self.c_idb, self.idb),
            (self.c_msu, self.msu), (self.c_miu, self.miu),
            (self.c_ones, self.onesr), (self.c_mj, self.mj_r),
            (self.c_cvw["q"], self.qcw_r), (self.c_cvw["k"], self.kcw_r),
            (self.c_cvw["v"], self.vcw_r),
            (self.c_cvb["q"], self.qcb_r), (self.c_cvb["k"], self.kcb_r),
            (self.c_cvb["v"], self.vcb_r),
            (self.c_lnw, self.lnw_r), (self.c_lnb, self.lnb_r),
            (self.c_ba, self.ba_r), (self.c_bb, self.bb_r),
            (self.c_sg, self.sg_r), (self.c_su, self.su_r), (self.c_sd, self.sd_r),
        ]
        ctoks = []
        for (st, sr), (dt_, drr) in loads:
            ctoks.append((sr, kb.dma("SY", st[:], dt_[:], reads=[drr],
                                     writes=[sr], ch="c")))
        final = ctoks[-1][1]
        for sr, _ in ctoks:
            sr.prod = final
        kb.op("DV", lambda e: e.memset(self.c_ones16[0][:], 1.0),
              writes=[self.c_ones16[1]])
        kb.op("DV", lambda e: e.memset(self.c_eps[0][:], LN_EPS),
              writes=[self.c_eps[1]])

    def emit_ln(self, src, dst, width, rows=128, cols=None, scr=None):
        kb = self.kb
        ct, cr = cols or self.lncols
        sct, scr_r = scr
        st_, sr_ = src
        dt_, drr = dst
        r = slice(0, rows)
        kb.op("DV", lambda e: e.tensor_reduce(ct[r, 0:1], st_[r, :], axis=AX.X,
                                              op=ALU.add),
              reads=[sr_], writes=[cr])
        kb.op("AC", lambda e: e.activation(sct[r, :], st_[r, :], AF.Square,
                                           accum_out=ct[r, 1:2]),
              reads=[sr_], writes=[scr_r, cr])
        kb.op("DV", lambda e: e.tensor_scalar(ct[r, 2:3], ct[r, 0:1], 1.0 / width,
                                              None, op0=ALU.mult),
              reads=[cr], writes=[cr])
        kb.op("DV", lambda e: e.tensor_tensor(ct[r, 3:4], ct[r, 2:3], ct[r, 2:3],
                                              op=ALU.mult),
              reads=[cr], writes=[cr])
        kb.op("DV", lambda e: e.tensor_scalar(ct[r, 4:5], ct[r, 1:2], 1.0 / width,
                                              ct[r, 3:4], op0=ALU.mult,
                                              op1=ALU.subtract),
              reads=[cr], writes=[cr])
        kb.op("AC", lambda e: e.activation(ct[r, 5:6], ct[r, 4:5], AF.Ln,
                                           bias=self.c_eps[0][r, 0:1]),
              reads=[cr, self.c_eps[1]], writes=[cr])
        kb.op("AC", lambda e: e.activation(ct[r, 6:7], ct[r, 5:6], AF.Exp,
                                           scale=-0.5),
              reads=[cr], writes=[cr])
        kb.op("DV", lambda e: e.tensor_scalar(dt_[r, :], st_[r, :], ct[r, 2:3],
                                              ct[r, 6:7], op0=ALU.subtract,
                                              op1=ALU.mult),
              reads=[sr_, cr], writes=[drr])

    def p1_xn(self):
        kb = self.kb
        self.p1_stack = ExitStack()
        xb = [self.sbuf(f"xb{i}", [128, H], F32, self.p1_stack) for i in range(2)]
        xn = [self.sbuf(f"xn{i}", [128, H], F32, self.p1_stack) for i in range(2)]
        lnscr = self.sbuf("lnscr1", [128, H], F32, self.p1_stack)
        xr, xres = self.x_r
        for ci in range(NCH + 1):
            tc = ci - 1
            bx, nx = xb[ci % 2], xn[ci % 2]
            kb.dma("SY", bx[0][:], xr[ci * 128:(ci + 1) * 128, :],
                   reads=[xres], writes=[bx[1]], ch=f"x{ci % 2}")
            self.emit_ln(bx, nx, H, scr=lnscr)
            for d in range(16):
                bk = self.bank("F")
                kb.op("PE", lambda e, bk=bk, nx=nx, d=d: e.transpose(
                    bk[0][:, :128], nx[0][:, d * 128:(d + 1) * 128],
                    self.c_idf[0][:]),
                    reads=[nx[1], self.c_idf[1]], writes=[bk[1]])
                if ci == 0:
                    kb.op("AC", lambda e, bk=bk, d=d: e.copy(
                        self.xnT_halo[0][:, d * 3:d * 3 + 3],
                        bk[0][:, 125:128]),
                        reads=[bk[1]], writes=[self.xnT_halo[1]])
                else:
                    kb.op("AC", lambda e, bk=bk, d=d, tc=tc: e.copy(
                        self.A1[d][0][:, tc * 128:(tc + 1) * 128],
                        bk[0][:, :128]),
                        reads=[bk[1]], writes=[self.A1[d][1]])
        kb.barrier()
        self.p1_stack.close()

    def proj(self, wname, conv, spill, silu, scale):
        kb = self.kb
        pst = ExitStack()
        wd, wres = getattr(self, wname)
        spt, spres = spill
        krs = [self.sbuf(f"kraw_{wname}{i}", [128, TPC + 3], F32, pst)
               for i in range(2)]
        cys = [self.sbuf(f"convy_{wname}{i}", [128, TPC], F32, pst)
               for i in range(2)]
        cts = [self.sbuf(f"convt_{wname}{i}", [128, TPC], F32, pst)
               for i in range(2)]
        kss = [self.sbuf(f"ksil_{wname}{i}", [128, TPC], BF16, pst)
               for i in range(2)]
        for i in range(16):
            tokw = kb.dma("GP", self.A2[i][0][:], wd[i * 128:(i + 1) * 128, :],
                          reads=[wres], writes=[self.A2[i][1]], ch="wgt")
        for i in range(16):
            self.A2[i][1].prod = tokw
        for j in range(16):
            kr, cy, ct_, ks = krs[j % 2], cys[j % 2], cts[j % 2], kss[j % 2]
            if PROJ_BARRIER & 1:
                kb.barrier()
            for tg in range(4):
                bk = self.bank("F")
                for i in range(16):
                    kb.op("PE", lambda e, bk=bk, i=i, j=j, tg=tg: e.matmul(
                        bk[0][:, :], self.A2[i][0][:, j * 128:(j + 1) * 128],
                        self.A1[i][0][:, tg * 512:(tg + 1) * 512],
                        start=(i == 0), stop=(i == 15)),
                        reads=[self.A2[i][1], self.A1[i][1]],
                        writes=([bk[1]] if i in (0, 15) else []))
                if PROJ_BARRIER & 2:
                    kb.barrier()
                if conv is not None:
                    kb.op("AC", lambda e, bk=bk, tg=tg: e.copy(
                        kr[0][:, 3 + tg * 512: 3 + (tg + 1) * 512], bk[0][:, :]),
                        reads=[bk[1]], writes=[kr[1]])
                else:
                    kb.op("AC", lambda e, bk=bk, tg=tg: e.activation(
                        ks[0][:, tg * 512:(tg + 1) * 512], bk[0][:, :],
                        AF.Sigmoid),
                        reads=[bk[1]], writes=[ks[1]])
            if conv is not None:
                bh = self.bank("F")
                for i in range(16):
                    kb.op("PE", lambda e, bh=bh, i=i, j=j: e.matmul(
                        bh[0][:, :3], self.A2[i][0][:, j * 128:(j + 1) * 128],
                        self.xnT_halo[0][:, i * 3:i * 3 + 3],
                        start=(i == 0), stop=(i == 15)),
                        reads=[self.A2[i][1], self.xnT_halo[1]],
                        writes=([bh[1]] if i in (0, 15) else []))
                kb.op("AC", lambda e, bh=bh: e.copy(kr[0][:, 0:3], bh[0][:, :3]),
                      reads=[bh[1]], writes=[kr[1]])
                if PROJ_BARRIER & 4:
                    kb.barrier()
                cw, cwres = self.c_cvw[conv]
                cb, cbres = self.c_cvb[conv]
                kb.op("DV", lambda e, j=j: e.tensor_scalar(
                    cy[0][:, :], kr[0][:, 0:TPC], cw[:, j * 4:j * 4 + 1],
                    cb[:, j:j + 1], op0=ALU.mult, op1=ALU.add),
                    reads=[kr[1], cwres, cbres], writes=[cy[1]])
                for jj in range(1, 4):
                    kb.op("AC", lambda e, j=j, jj=jj: e.activation(
                        ct_[0][:, :], kr[0][:, jj:jj + TPC], AF.Copy,
                        scale=cw[:, j * 4 + jj:j * 4 + jj + 1]),
                        reads=[kr[1], cwres], writes=[ct_[1]])
                    kb.op("DV", lambda e: e.tensor_tensor(
                        cy[0][:, :], cy[0][:, :], ct_[0][:, :], op=ALU.add),
                        reads=[cy[1], ct_[1]], writes=[cy[1]])
                if scale != 1.0:
                    kb.op("AC", lambda e: e.activation(cy[0][:, :], cy[0][:, :],
                                                       AF.Silu),
                          reads=[cy[1]], writes=[cy[1]])
                    kb.op("DV", lambda e: e.tensor_scalar(
                        ks[0][:, :], cy[0][:, :], scale, None, op0=ALU.mult),
                        reads=[cy[1]], writes=[ks[1]])
                else:
                    kb.op("AC", lambda e: e.activation(ks[0][:, :], cy[0][:, :],
                                                       AF.Silu),
                          reads=[cy[1]], writes=[ks[1]])
            if PROJ_BARRIER & 8:
                kb.barrier()
            kb.dma("SY", spt[j], ks[0][:, :], reads=[ks[1]], writes=[spres],
                   ch=f"sp_{wname}{j % 2}")
        for par in range(2):
            c = kb.chan(f"sp_{wname}{par}")
            kb.SY.wait_tok((c, c.count))
        spres.prod = None
        pst.close()

    def p3_ab(self):
        kb = self.kb
        p3st = ExitStack()
        ss = p3st
        wab = self.sbuf("wab", [128, 2 * 16 * 16], BF16, ss)
        wst = ExitStack()
        watt = self.sbuf("watt", [NH, H], F32, wst)
        wbtt = self.sbuf("wbtt", [NH, H], F32, wst)
        kb.dma("SY", watt[0][:], self.WaT[0][:], reads=[self.WaT[1]],
               writes=[watt[1]], ch="wa")
        kb.dma("SY", wbtt[0][:], self.WbT[0][:], reads=[self.WbT[1]],
               writes=[wbtt[1]], ch="wb")
        for ab, wt in enumerate((watt, wbtt)):
            for c in range(16):
                bk = self.bank("F")
                kb.op("PE", lambda e, bk=bk, wt=wt, c=c: e.transpose(
                    bk[0][:, :16], wt[0][:, c * 128:(c + 1) * 128],
                    self.c_idf[0][0:16, 0:16]),
                    reads=[wt[1], self.c_idf[1]], writes=[bk[1]])
                kb.op("AC", lambda e, bk=bk, ab=ab, c=c: e.copy(
                    wab[0][:, (ab * 16 + c) * 16:(ab * 16 + c) * 16 + 16],
                    bk[0][:, :16]),
                    reads=[bk[1]], writes=[wab[1]])
        kb.barrier()
        wst.close()
        zst = ExitStack()
        laF = self.sbuf("laF", [NH, TPC], F32, zst)
        bsF = self.sbuf("bsF", [NH, TPC], F32, zst)
        lbF = self.sbuf("lbF", [NH, TPC], F32, zst)
        for ab in range(2):
            dst_s = laF if ab == 0 else bsF
            dst_l = laF if ab == 0 else lbF
            bias = self.c_ba if ab == 0 else self.c_bb
            for tg in range(4):
                bk = self.bank("F")
                for c in range(16):
                    kb.op("PE", lambda e, bk=bk, ab=ab, c=c, tg=tg: e.matmul(
                        bk[0][0:16, :],
                        wab[0][:, (ab * 16 + c) * 16:(ab * 16 + c) * 16 + 16],
                        self.A1[c][0][:, tg * 512:(tg + 1) * 512],
                        start=(c == 0), stop=(c == 15)),
                        reads=[wab[1], self.A1[c][1]],
                        writes=([bk[1]] if c in (0, 15) else []))
                kb.op("AC", lambda e, bk=bk, d=dst_s, b=bias, tg=tg: e.activation(
                    d[0][:, tg * 512:(tg + 1) * 512], bk[0][0:16, :], AF.Sigmoid,
                    bias=b[0][:, :]),
                    reads=[bk[1], bias[1]], writes=[dst_s[1]])
            kb.op("AC", lambda e, s=dst_s, l=dst_l: e.activation(
                l[0][:, :], s[0][:, :], AF.Ln),
                reads=[dst_s[1]], writes=[dst_l[1]])
        caF = self.sbuf("caF", [NH, TPC], F32, zst)
        sc16 = self.sbuf("sc16", [NH, 128], F32, zst)
        sc2 = self.sbuf("sc2", [NH, 2], F32, zst)
        latok = self.sbuf("latok", [128, 16], F32, zst)
        bstok = self.sbuf("bstok", [128, 16], F32, zst)
        catok = self.sbuf("catok", [128, 16], F32, zst)
        cash = self.sbuf("cash", [128, 16], F32, zst)
        ecash = self.sbuf("ecash", [128, 16], F32, zst)
        cp, cpr = self.colpack
        for tc in range(NCH):
            sl = slice(tc * 128, (tc + 1) * 128)
            base = tc * 6 * NH
            col = lambda q: cp[:, base + q * NH: base + (q + 1) * NH]
            b1 = self.bank("F")
            kb.op("PE", lambda e, b=b1, sl=sl: e.transpose(
                b[0][:, :16], laF[0][:, sl], self.c_idf[0][0:16, 0:16]),
                reads=[laF[1], self.c_idf[1]], writes=[b1[1]])
            kb.op("AC", lambda e, b=b1: e.copy(latok[0][:], b[0][:, :16]),
                  reads=[b1[1]], writes=[latok[1]])
            b2 = self.bank("F")
            kb.op("PE", lambda e, b=b2, sl=sl: e.transpose(
                b[0][:, :16], bsF[0][:, sl], self.c_idf[0][0:16, 0:16]),
                reads=[bsF[1], self.c_idf[1]], writes=[b2[1]])
            kb.op("AC", lambda e, b=b2: e.copy(bstok[0][:], b[0][:, :16]),
                  reads=[b2[1]], writes=[bstok[1]])
            kb.op("DV", lambda e: e.tensor_copy(col(0), bstok[0][:]),
                  reads=[bstok[1]], writes=[cpr])
            b3 = self.bank("F")
            kb.op("PE", lambda e, b=b3: e.matmul(
                b[0][:, 0:16], self.c_miu[0][:], latok[0][:],
                start=True, stop=True),
                reads=[self.c_miu[1], latok[1]], writes=[b3[1]])
            kb.op("AC", lambda e, b=b3: e.copy(catok[0][:], b[0][:, 0:16]),
                  reads=[b3[1]], writes=[catok[1]])
            kb.op("DV", lambda e: e.tensor_scalar(col(4), catok[0][:], -1.0,
                                                  None, op0=ALU.mult),
                  reads=[catok[1]], writes=[cpr])
            kb.op("AC", lambda e: e.activation(col(2), catok[0][:], AF.Exp),
                  reads=[catok[1]], writes=[cpr])
            kb.op("DV", lambda e: e.tensor_tensor(cash[0][:], catok[0][:],
                                                  latok[0][:], op=ALU.subtract),
                  reads=[catok[1], latok[1]], writes=[cash[1]])
            kb.op("AC", lambda e: e.activation(ecash[0][:], cash[0][:], AF.Exp),
                  reads=[cash[1]], writes=[ecash[1]])
            kb.op("DV", lambda e: e.tensor_tensor(cash[0][:], ecash[0][:],
                                                  bstok[0][:], op=ALU.mult),
                  reads=[ecash[1], bstok[1]], writes=[cash[1]])
            kb.op("DV", lambda e: e.tensor_scalar(col(1), cash[0][:], -1.0,
                                                  None, op0=ALU.mult),
                  reads=[cash[1]], writes=[cpr])
            b4 = self.bank("F")
            kb.op("PE", lambda e, b=b4: e.transpose(
                b[0][0:16, :128], catok[0][:], self.c_idf[0][:]),
                reads=[catok[1], self.c_idf[1]], writes=[b4[1]])
            kb.op("AC", lambda e, b=b4, sl=sl: e.copy(caF[0][:, sl],
                                                      b[0][0:16, :128]),
                  reads=[b4[1]], writes=[caF[1]])
            kb.op("DV", lambda e, sl=sl: e.tensor_reduce(
                sc2[0][:, 0:1], laF[0][:, sl], axis=AX.X, op=ALU.add),
                reads=[laF[1]], writes=[sc2[1]])
            kb.op("AC", lambda e, sl=sl: e.activation(
                sc16[0][:, :], caF[0][:, sl], AF.Exp, bias=sc2[0][:, 0:1],
                scale=-1.0),
                reads=[caF[1], sc2[1]], writes=[sc16[1]])
            b5 = self.bank("F")
            kb.op("PE", lambda e, b=b5: e.transpose(
                b[0][:, :16], sc16[0][:, :], self.c_idf[0][0:16, 0:16]),
                reads=[sc16[1], self.c_idf[1]], writes=[b5[1]])
            kb.op("AC", lambda e, b=b5: e.copy(col(3), b[0][:, :16]),
                  reads=[b5[1]], writes=[cpr])
            kb.op("AC", lambda e: e.activation(sc2[0][:, 1:2], sc2[0][:, 0:1],
                                               AF.Exp),
                  reads=[sc2[1]], writes=[sc2[1]])
            kb.op("DV", lambda e: e.tensor_scalar(sc16[0][:, :],
                                                  self.c_ones16[0][:],
                                                  sc2[0][:, 1:2], None,
                                                  op0=ALU.mult),
                  reads=[self.c_ones16[1], sc2[1]], writes=[sc16[1]])
            b6 = self.bank("F")
            kb.op("PE", lambda e, b=b6: e.transpose(
                b[0][:, :16], sc16[0][:, :], self.c_idf[0][0:16, 0:16]),
                reads=[sc16[1], self.c_idf[1]], writes=[b6[1]])
            kb.op("AC", lambda e, b=b6: e.copy(col(5), b[0][:, :16]),
                  reads=[b6[1]], writes=[cpr])
        r2F = bsF
        kb.op("DV", lambda e: e.tensor_tensor(r2F[0][:], caF[0][:], laF[0][:],
                                              op=ALU.subtract),
              reads=[caF[1], laF[1]], writes=[r2F[1]])
        kb.op("DV", lambda e: e.tensor_tensor(r2F[0][:], r2F[0][:], lbF[0][:],
                                              op=ALU.add),
              reads=[r2F[1], lbF[1]], writes=[r2F[1]])
        rd, rdr = self.rows_d
        kb.dma("SY", rd[0], r2F[0][:], reads=[r2F[1]], writes=[rdr],
               ch="sp_r")
        kb.dma("SY", rd[1], caF[0][:], reads=[caF[1]], writes=[rdr],
               ch="sp_r")
        kb.barrier()
        zst.close()
        p3st.close()

    def ccol(self, tc, q, h):
        return self.colpack[0][:, tc * 6 * NH + q * NH + h:
                               tc * 6 * NH + q * NH + h + 1]

    def p4_phase1(self):
        kb = self.kb
        st = ExitStack()
        self.p4_stack = st
        PTa = [self.sbuf(f"PTa{h}", [128, 128], BF16, st) for h in range(NH)]
        BTa = [self.sbuf(f"BTa{h}", [128, 128], BF16, st) for h in range(NH)]
        S = []
        for p in range(2):
            d = {}
            d["khb"] = self.sbuf(f"p1kh{p}", [128, TPC], BF16, st)
            d["vhb"] = self.sbuf(f"p1vh{p}", [128, TPC], BF16, st)
            d["rwb"] = self.sbuf(f"p1row{p}", [1, TPC], F32, st)
            d["kkts"] = self.sbuf(f"s_kkts{p}", [128, 128], F32, st)
            d["rc"] = self.sbuf(f"s_rc{p}", [128, 128], F32, st)
            d["e1"] = self.sbuf(f"s_e1{p}", [128, 128], F32, st)
            d["lt"] = self.sbuf(f"s_lt{p}", [128, 128], BF16, st)
            d["kpr"] = self.sbuf(f"s_kpr{p}", [128, 128], BF16, st)
            d["rhs0"] = self.sbuf(f"s_rhs0{p}", [128, 256], F32, st)
            d["xb"] = [self.sbuf(f"s_xb{p}_{i}", [128, 256], BF16, st)
                       for i in range(2)]
            d["ptile"] = self.sbuf(f"s_ptile{p}", [128, 128], BF16, st)
            d["pcs"] = self.sbuf(f"s_pc{p}", [128, 128], BF16, st)
            S.append(d)
        cpr = self.colpack[1]
        for h in range(NH):
            kb.op("DV", lambda e, h=h: e.tensor_copy(PTa[h][0][:],
                                                     self.c_idb[0][:]),
                  reads=[self.c_idb[1]], writes=[PTa[h][1]])
            kb.op("DV", lambda e, h=h: e.memset(BTa[h][0][:], 0.0),
                  writes=[BTa[h][1]])
        rd, rdr = self.rows_d

        def load_head(h, d):
            kb.dma("SY", d["khb"][0][:], self.kT_s[0][h],
                   reads=[self.kT_s[1]], writes=[d["khb"][1]],
                   ch=f"kh{h % 2}")
            kb.dma("SY", d["vhb"][0][:], self.vT_s[0][h],
                   reads=[self.vT_s[1]], writes=[d["vhb"][1]],
                   ch=f"vh{h % 2}")
            kb.dma("SY", d["rwb"][0][:], rd[0:1, h, :],
                   reads=[rdr], writes=[d["rwb"][1]], ch=f"rw{h % 2}")

        def chunk1(h, tc, d):
            sl = slice(tc * 128, (tc + 1) * 128)
            kT = (d["khb"][0][:, sl], d["khb"][1])
            vT = (d["vhb"][0][:, sl], d["vhb"][1])
            r2row = (d["rwb"][0][0:1, sl], d["rwb"][1])
            kkts, rc, e1 = d["kkts"], d["rc"], d["e1"]
            lt, kpr, rhs0 = d["lt"], d["kpr"], d["rhs0"]
            xb, ptile, pcs = d["xb"], d["ptile"], d["pcs"]
            aci = rc
            bk1 = self.bank("F")
            kb.op("PE", lambda e, b=bk1, kT=kT: e.matmul(
                b[0][:, :128], kT[0], kT[0], start=True, stop=True),
                reads=[kT[1]], writes=[bk1[1]])
            bk2 = self.bank("F")
            kb.op("PE", lambda e, b=bk2, r2=r2row: e.matmul(
                b[0][:, :128], self.c_ones[0][0:1, :], r2[0],
                start=True, stop=True),
                reads=[self.c_ones[1], r2row[1]], writes=[bk2[1]])
            bt1 = self.bank("B")
            kb.op("PE", lambda e, b=bt1, kT=kT: e.transpose(
                b[0][:, :128], kT[0], self.c_idb[0][:]),
                reads=[kT[1], self.c_idb[1]], writes=[bt1[1]])
            bt2 = self.bank("B")
            kb.op("PE", lambda e, b=bt2, vT=vT: e.transpose(
                b[0][:, :128], vT[0], self.c_idb[0][:]),
                reads=[vT[1], self.c_idb[1]], writes=[bt2[1]])
            kb.op("DV", lambda e, b=bk1: e.tensor_tensor(
                kkts[0][:], b[0][:, :128], self.c_msu[0][:], op=ALU.mult),
                reads=[bk1[1], self.c_msu[1]], writes=[kkts[1]])
            kb.op("DV", lambda e, b=bk2, tc=tc, h=h: e.tensor_scalar(
                rc[0][:], b[0][:, :128], self.ccol(tc, 4, h), 0.0,
                op0=ALU.add, op1=ALU.min),
                reads=[bk2[1], cpr], writes=[rc[1]])
            kb.op("AC", lambda e: e.activation(e1[0][:], rc[0][:], AF.Exp),
                  reads=[rc[1]], writes=[e1[1]])
            kb.op("DV", lambda e: e.tensor_tensor(lt[0][:], e1[0][:],
                                                  kkts[0][:], op=ALU.mult),
                  reads=[e1[1], kkts[1]], writes=[lt[1]])
            kb.op("AC", lambda e, b=bt1, tc=tc, h=h: e.activation(
                kpr[0][:], b[0][:, :128], AF.Copy,
                scale=self.ccol(tc, 3, h)),
                reads=[bt1[1], cpr], writes=[kpr[1]])
            kb.op("AC", lambda e, b=bt2, tc=tc, h=h: e.activation(
                rhs0[0][:, 0:128], b[0][:, :128], AF.Copy,
                scale=self.ccol(tc, 0, h)),
                reads=[bt2[1], cpr], writes=[rhs0[1]])
            kb.op("DV", lambda e, b=bt1, tc=tc, h=h: e.tensor_scalar(
                rhs0[0][:, 128:256], b[0][:, :128], self.ccol(tc, 1, h),
                -1.0, op0=ALU.mult, op1=ALU.mult),
                reads=[bt1[1], cpr], writes=[rhs0[1]])
            kb.op("DV", lambda e: e.tensor_copy(xb[0][0][:], rhs0[0][:]),
                  reads=[rhs0[1]], writes=[xb[0][1]])
            for it in range(NEU):
                sx, dx = xb[it % 2], xb[(it + 1) % 2]
                bn = self.bank("F")
                kb.op("PE", lambda e, b=bn, s=sx: e.matmul(
                    b[0][:, 0:256], lt[0][:], s[0][:],
                    start=True, stop=True),
                    reads=[lt[1], sx[1]], writes=[bn[1]])
                kb.op("DV", lambda e, b=bn, dx=dx: e.tensor_tensor(
                    dx[0][:], rhs0[0][:], b[0][:, 0:256], op=ALU.subtract),
                    reads=[rhs0[1], bn[1]], writes=[dx[1]])
            xf = xb[NEU % 2]
            bk3 = self.bank("F")
            kb.op("PE", lambda e, b=bk3, x=xf: e.matmul(
                b[0][:, :128], kpr[0][:], x[0][:, 128:256],
                start=True, stop=True),
                reads=[kpr[1], xf[1]], writes=[bk3[1]])
            kb.op("DV", lambda e, tc=tc, h=h: e.tensor_scalar(
                aci[0][:], self.c_idf[0][:], self.ccol(tc, 5, h), None,
                op0=ALU.mult),
                reads=[self.c_idf[1], cpr], writes=[aci[1]])
            kb.op("DV", lambda e, b=bk3: e.tensor_tensor(
                ptile[0][:], aci[0][:], b[0][:, :128], op=ALU.subtract),
                reads=[aci[1], bk3[1]], writes=[ptile[1]])
            bt3 = self.bank("B")
            kb.op("PE", lambda e, b=bt3: e.transpose(
                b[0][:, :128], ptile[0][:], self.c_idb[0][:]),
                reads=[ptile[1], self.c_idb[1]], writes=[bt3[1]])
            kb.op("AC", lambda e, b=bt3: e.copy(pcs[0][:], b[0][:, :128]),
                  reads=[bt3[1]], writes=[pcs[1]])
            bk4 = self.bank("F")
            kb.op("PE", lambda e, b=bk4, x=xf: e.matmul(
                b[0][:, :128], kpr[0][:], x[0][:, 0:128],
                start=True, stop=False),
                reads=[kpr[1], xf[1]], writes=[bk4[1]])
            kb.op("PE", lambda e, b=bk4, h=h: e.matmul(
                b[0][:, :128], pcs[0][:], BTa[h][0][:],
                start=False, stop=True),
                reads=[pcs[1], BTa[h][1]], writes=[bk4[1]])
            bk5 = self.bank("F")
            kb.op("PE", lambda e, b=bk5, h=h: e.matmul(
                b[0][:, :128], pcs[0][:], PTa[h][0][:],
                start=True, stop=True),
                reads=[pcs[1], PTa[h][1]], writes=[bk5[1]])
            kb.op("AC", lambda e, b=bk4, h=h: e.copy(BTa[h][0][:],
                                                     b[0][:, :128]),
                  reads=[bk4[1]], writes=[BTa[h][1]])
            kb.op("AC", lambda e, b=bk5, h=h: e.copy(PTa[h][0][:],
                                                     b[0][:, :128]),
                  reads=[bk5[1]], writes=[PTa[h][1]])

        for hp in range(0, NH, 2):
            load_head(hp, S[0])
            load_head(hp + 1, S[1])
            for tc in range(NCH):
                chunk1(hp, tc, S[0])
                chunk1(hp + 1, tc, S[1])
        cci, ccir = self.cc_in
        pcs = S[0]["pcs"]
        for h in range(NH):
            bt = self.bank("B")
            kb.op("PE", lambda e, b=bt, h=h: e.transpose(
                b[0][:, :128], PTa[h][0][:], self.c_idb[0][:]),
                reads=[PTa[h][1], self.c_idb[1]], writes=[bt[1]])
            kb.op("AC", lambda e, b=bt: e.copy(pcs[0][:], b[0][:, :128]),
                  reads=[bt[1]], writes=[pcs[1]])
            kb.dma("SY", cci[h, 0], pcs[0][:], reads=[pcs[1]], writes=[ccir],
                   ch="sp_cc")
            kb.dma("SY", cci[h, 1], BTa[h][0][:], reads=[BTa[h][1]],
                   writes=[ccir], ch="sp_cc")
        cco, ccor = self.cc_out
        deps = ccir.deps_for_read() + ccor.deps_for_write()
        kb._wait_deps(kb.GP, deps)
        ci = self.nc.gpsimd.collective_compute(
            "AllGather", ALU.bypass,
            replica_groups=[[0, 1, 2, 3], [4, 5, 6, 7]],
            ins=[cci[:]], outs=[cco[:]])
        kb.cc_count += 1
        ci.then_inc(kb.CC, 1)
        kb.barrier()
        st.close()

    def p6_compose(self):
        kb = self.kb
        st = ExitStack()
        pg = self.sbuf("pg", [128, 128], BF16, st)
        bg = self.sbuf("bg", [128, 128], BF16, st)
        t1 = self.sbuf("ct1", [128, 128], F32, st)
        cco, ccor = self.cc_out
        mjt, mjr = self.c_mj
        kb.SY.obj.wait_ge(kb.CC, kb.cc_count)
        for h in range(NH):
            kb.op("DV", lambda e, h=h: e.memset(self.S0T[h][0][:], 0.0),
                  writes=[self.S0T[h][1]])
            for j in range(3):
                kb.dma("SY", pg[0][:], cco[j, h, 0], reads=[ccor],
                       writes=[pg[1]], ch="pg")
                kb.dma("SY", bg[0][:], cco[j, h, 1], reads=[ccor],
                       writes=[bg[1]], ch="bg")
                kb.op("AC", lambda e, h=h: e.copy(self.S0b[h][0][:],
                                                  self.S0T[h][0][:]),
                      reads=[self.S0T[h][1]], writes=[self.S0b[h][1]])
                bk = self.bank("F")
                kb.op("PE", lambda e, b=bk, h=h: e.matmul(
                    b[0][:, :128], pg[0][:], self.S0b[h][0][:],
                    start=True, stop=True),
                    reads=[pg[1], self.S0b[h][1]], writes=[bk[1]])
                kb.op("DV", lambda e, b=bk, j=j: e.tensor_scalar(
                    t1[0][:], b[0][:, :128], mjt[:, j:j + 1], None,
                    op0=ALU.mult),
                    reads=[bk[1], mjr], writes=[t1[1]])
                kb.op("DV", lambda e, h=h, j=j: e.tensor_scalar(
                    self.S0T[h][0][:], self.S0T[h][0][:], mjt[:, 3 + j:4 + j],
                    None, op0=ALU.mult),
                    reads=[self.S0T[h][1], mjr], writes=[self.S0T[h][1]])
                kb.op("DV", lambda e, h=h: e.tensor_tensor(
                    self.S0T[h][0][:], self.S0T[h][0][:], t1[0][:], op=ALU.add),
                    reads=[self.S0T[h][1], t1[1]], writes=[self.S0T[h][1]])
                kb.op("DV", lambda e, j=j: e.tensor_scalar(
                    t1[0][:], bg[0][:], mjt[:, j:j + 1], None, op0=ALU.mult),
                    reads=[bg[1], mjr], writes=[t1[1]])
                kb.op("DV", lambda e, h=h: e.tensor_tensor(
                    self.S0T[h][0][:], self.S0T[h][0][:], t1[0][:], op=ALU.add),
                    reads=[self.S0T[h][1], t1[1]], writes=[self.S0T[h][1]])
            kb.op("AC", lambda e, h=h: e.copy(self.S0b[h][0][:],
                                              self.S0T[h][0][:]),
                  reads=[self.S0T[h][1]], writes=[self.S0b[h][1]])
        kb.barrier()
        st.close()

    def p7_phase2(self):
        kb = self.kb
        st = ExitStack()
        self.p7_stack = st
        rwt = [self.sbuf(f"p2row{i}", [65, 2 * TPC], F32, st)
               for i in range(2)]
        S = []
        for p in range(4):
            d = {}
            for i, nm in enumerate(("kh", "vh", "qh", "gh")):
                d[nm] = self.A2[4 * p + i]
            rt = rwt[p // 2]
            rbase = 64 * (p % 2)
            d["rw"] = ((rt[0], rbase), rt[1])
            for nm, shp, dt in (("kkts", [128, 128], BF16),
                                ("kqts", [128, 128], BF16),
                                ("rc1", [128, 128], F32),
                                ("rc2", [128, 128], F32),
                                ("lt", [128, 128], BF16),
                                ("mqt", [128, 128], BF16),
                                ("kpr", [128, 128], BF16),
                                ("bv", [128, 128], F32),
                                ("rhs", [128, 128], F32),
                                ("ols", [128, 128], F32),
                                ("otok", [128, 128], F32),
                                ("on", [128, 128], F32),
                                ("oc", [128, 8], F32),
                                ("rhst", [128, 128], F32)):
                d[nm] = self.sbuf(f"t_{nm}{p}", shp, dt, st)
            d["xb"] = [self.sbuf(f"t_xb{p}_{i}", [128, 128], BF16, st)
                       for i in range(2)]
            S.append(d)
        cpr = self.colpack[1]
        rd, rdr = self.rows_d

        def load_head(h, d):
            p4_ = h % 4
            kb.dma("SY", d["kh"][0][:], self.kT_s[0][h],
                   reads=[self.kT_s[1]], writes=[d["kh"][1]], ch=f"kh{p4_}")
            kb.dma("SY", d["vh"][0][:], self.vT_s[0][h],
                   reads=[self.vT_s[1]], writes=[d["vh"][1]], ch=f"vh{p4_}")
            kb.dma("SY", d["qh"][0][:], self.qT_s[0][h],
                   reads=[self.qT_s[1]], writes=[d["qh"][1]], ch=f"qh{p4_}")
            kb.dma("SY", d["gh"][0][:], self.gT_s[0][h],
                   reads=[self.gT_s[1]], writes=[d["gh"][1]], ch=f"gh{p4_}")
            (rt, rbase), rres = d["rw"]
            kb.dma("SY", rt[rbase:rbase + 1, 0:TPC], rd[0:1, h, :],
                   reads=[rdr], writes=[rres], ch=f"rw{p4_}")
            kb.dma("SY", rt[rbase:rbase + 1, TPC:2 * TPC], rd[1:2, h, :],
                   reads=[rdr], writes=[rres], ch=f"rw{p4_}")

        def chunk2(h, tc, d, pool=None):
            sl = slice(tc * 128, (tc + 1) * 128)
            kT = (d["kh"][0][:, sl], d["kh"][1])
            vT = (d["vh"][0][:, sl], d["vh"][1])
            qT = (d["qh"][0][:, sl], d["qh"][1])
            gT = (d["gh"][0][:, sl], d["gh"][1])
            (rt, rbase), rres = d["rw"]
            r2row = (rt[rbase:rbase + 1, sl], rres)
            r3row = (rt[rbase:rbase + 1, TPC + tc * 128: TPC + (tc + 1) * 128],
                     rres)
            kkts, kqts = d["kkts"], d["kqts"]
            rc1, rc2 = d["rc1"], d["rc2"]
            e1, e2 = rc1, rc2
            lt, mqt, kpr, bv = d["lt"], d["mqt"], d["kpr"], d["bv"]
            rhs, ols, otok, on = d["rhs"], d["ols"], d["otok"], d["on"]
            oc, rhst, xb = d["oc"], d["rhst"], d["xb"]
            oscr = d["on"]
            ones_sl = self.c_ones[0][rbase:rbase + 1, :]
            b1 = self.bank("F")
            kb.op("PE", lambda e, b=b1, kT=kT: e.matmul(
                b[0][:, 0:128], kT[0], kT[0], start=True, stop=True),
                reads=[kT[1]], writes=[b1[1]])
            kb.op("PE", lambda e, b=b1, kT=kT, qT=qT: e.matmul(
                b[0][:, 128:256], kT[0], qT[0], start=True, stop=True),
                reads=[kT[1], qT[1]], writes=[b1[1]])
            kb.op("DV", lambda e, b=b1: e.tensor_tensor(
                kkts[0][:], b[0][:, 0:128], self.c_msu[0][:], op=ALU.mult),
                reads=[b1[1], self.c_msu[1]], writes=[kkts[1]])
            kb.op("DV", lambda e, b=b1: e.tensor_tensor(
                kqts[0][:], b[0][:, 128:256], self.c_miu[0][:], op=ALU.mult),
                reads=[b1[1], self.c_miu[1]], writes=[kqts[1]])
            yield
            b2 = self.bank("F")
            kb.op("PE", lambda e, b=b2, r=r2row, o=ones_sl: e.matmul(
                b[0][:, 0:128], o, r[0], start=True, stop=True),
                reads=[self.c_ones[1], r2row[1]], writes=[b2[1]])
            kb.op("PE", lambda e, b=b2, r=r3row, o=ones_sl: e.matmul(
                b[0][:, 128:256], o, r[0], start=True, stop=True),
                reads=[self.c_ones[1], r3row[1]], writes=[b2[1]])
            kb.op("DV", lambda e, b=b2, tc=tc, h=h: e.tensor_scalar(
                rc1[0][:], b[0][:, 0:128], self.ccol(tc, 4, h), 0.0,
                op0=ALU.add, op1=ALU.min),
                reads=[b2[1], cpr], writes=[rc1[1]])
            kb.op("DV", lambda e, b=b2, tc=tc, h=h: e.tensor_scalar(
                rc2[0][:], b[0][:, 128:256], self.ccol(tc, 4, h), 0.0,
                op0=ALU.add, op1=ALU.min),
                reads=[b2[1], cpr], writes=[rc2[1]])
            yield
            bt1 = self.bank("B")
            kb.op("PE", lambda e, b=bt1, kT=kT: e.transpose(
                b[0][:, :128], kT[0], self.c_idb[0][:]),
                reads=[kT[1], self.c_idb[1]], writes=[bt1[1]])
            kb.op("AC", lambda e, b=bt1, tc=tc, h=h: e.activation(
                kpr[0][:], b[0][:, :128], AF.Copy,
                scale=self.ccol(tc, 3, h)),
                reads=[bt1[1], cpr], writes=[kpr[1]])
            bt2 = self.bank("B")
            kb.op("PE", lambda e, b=bt2, vT=vT: e.transpose(
                b[0][:, :128], vT[0], self.c_idb[0][:]),
                reads=[vT[1], self.c_idb[1]], writes=[bt2[1]])
            kb.op("AC", lambda e, b=bt2, tc=tc, h=h: e.activation(
                bv[0][:], b[0][:, :128], AF.Copy,
                scale=self.ccol(tc, 0, h)),
                reads=[bt2[1], cpr], writes=[bv[1]])
            yield
            kb.op("AC", lambda e: e.activation(e1[0][:], rc1[0][:], AF.Exp),
                  reads=[rc1[1]], writes=[e1[1]])
            kb.op("AC", lambda e: e.activation(e2[0][:], rc2[0][:], AF.Exp),
                  reads=[rc2[1]], writes=[e2[1]])
            kb.op("DV", lambda e: e.tensor_tensor(lt[0][:], e1[0][:],
                                                  kkts[0][:], op=ALU.mult),
                  reads=[e1[1], kkts[1]], writes=[lt[1]])
            kb.op("DV", lambda e: e.tensor_tensor(mqt[0][:], e2[0][:],
                                                  kqts[0][:], op=ALU.mult),
                  reads=[e2[1], kqts[1]], writes=[mqt[1]])
            yield
            b3 = self.bank("F")
            kb.op("PE", lambda e, b=b3, kT=kT, h=h: e.matmul(
                b[0][:, :128], kT[0], self.S0b[h][0][:],
                start=True, stop=True),
                reads=[kT[1], self.S0b[h][1]], writes=[b3[1]])
            kb.op("AC", lambda e, b=b3, tc=tc, h=h: e.activation(
                rhst[0][:], b[0][:, :128], AF.Copy,
                scale=self.ccol(tc, 1, h)),
                reads=[b3[1], cpr], writes=[rhst[1]])
            kb.op("DV", lambda e: e.tensor_tensor(
                rhs[0][:], rhst[0][:], bv[0][:], op=ALU.add),
                reads=[rhst[1], bv[1]], writes=[rhs[1]])
            kb.op("DV", lambda e: e.tensor_copy(xb[0][0][:], rhs[0][:]),
                  reads=[rhs[1]], writes=[xb[0][1]])
            yield
            for it in range(NEU):
                sx, dx = xb[it % 2], xb[(it + 1) % 2]
                bn = self.bank("F")
                kb.op("PE", lambda e, b=bn, s=sx: e.matmul(
                    b[0][:, 0:128], lt[0][:], s[0][:],
                    start=True, stop=True),
                    reads=[lt[1], sx[1]], writes=[bn[1]])
                kb.op("DV", lambda e, b=bn, dx=dx: e.tensor_tensor(
                    dx[0][:], rhs[0][:], b[0][:, 0:128], op=ALU.subtract),
                    reads=[rhs[1], bn[1]], writes=[dx[1]])
                yield
            U = xb[NEU % 2]
            b4 = self.bank("F")
            kb.op("PE", lambda e, b=b4, U=U: e.matmul(
                b[0][:, 0:128], mqt[0][:], U[0][:], start=True, stop=True),
                reads=[mqt[1], U[1]], writes=[b4[1]])
            kb.op("PE", lambda e, b=b4, qT=qT, h=h: e.matmul(
                b[0][:, 128:256], qT[0], self.S0b[h][0][:],
                start=True, stop=True),
                reads=[qT[1], self.S0b[h][1]], writes=[b4[1]])
            kb.op("AC", lambda e, b=b4: e.copy(ols[0][:], b[0][:, 0:128]),
                  reads=[b4[1]], writes=[ols[1]])
            kb.op("AC", lambda e, b=b4, tc=tc, h=h: e.activation(
                rhst[0][:], b[0][:, 128:256], AF.Copy,
                scale=self.ccol(tc, 2, h)),
                reads=[b4[1], cpr], writes=[rhst[1]])
            kb.op("DV", lambda e: e.tensor_tensor(
                otok[0][:], rhst[0][:], ols[0][:], op=ALU.add),
                reads=[rhst[1], ols[1]], writes=[otok[1]])
            yield
            b5 = self.bank("F")
            kb.op("PE", lambda e, b=b5, U=U: e.matmul(
                b[0][:, :128], kpr[0][:], U[0][:], start=True, stop=True),
                reads=[kpr[1], U[1]], writes=[b5[1]])
            kb.op("DV", lambda e, tc=tc, h=h: e.tensor_scalar(
                self.S0T[h][0][:], self.S0T[h][0][:], self.ccol(tc, 5, h),
                None, op0=ALU.mult),
                reads=[self.S0T[h][1], cpr], writes=[self.S0T[h][1]])
            kb.op("DV", lambda e, b=b5, h=h: e.tensor_tensor(
                self.S0T[h][0][:], self.S0T[h][0][:], b[0][:, :128],
                op=ALU.add),
                reads=[self.S0T[h][1], b5[1]], writes=[self.S0T[h][1]])
            kb.op("AC", lambda e, h=h: e.copy(self.S0b[h][0][:],
                                              self.S0T[h][0][:]),
                  reads=[self.S0T[h][1]], writes=[self.S0b[h][1]])
            yield
            kb.op("DV", lambda e: e.tensor_reduce(oc[0][:, 0:1], otok[0][:],
                                                  axis=AX.X, op=ALU.add),
                  reads=[otok[1]], writes=[oc[1]])
            kb.op("AC", lambda e: e.activation(
                oscr[0][:], otok[0][:], AF.Square,
                accum_out=oc[0][:, 1:2]),
                reads=[otok[1]], writes=[oscr[1], oc[1]])
            yield
            kb.op("DV", lambda e: e.tensor_scalar(oc[0][:, 2:3],
                                                  oc[0][:, 0:1], 1.0 / D,
                                                  None, op0=ALU.mult),
                  reads=[oc[1]], writes=[oc[1]])
            kb.op("DV", lambda e: e.tensor_tensor(oc[0][:, 3:4],
                                                  oc[0][:, 2:3],
                                                  oc[0][:, 2:3],
                                                  op=ALU.mult),
                  reads=[oc[1]], writes=[oc[1]])
            kb.op("DV", lambda e: e.tensor_scalar(oc[0][:, 4:5],
                                                  oc[0][:, 1:2], 1.0 / D,
                                                  oc[0][:, 3:4],
                                                  op0=ALU.mult,
                                                  op1=ALU.subtract),
                  reads=[oc[1]], writes=[oc[1]])
            kb.op("AC", lambda e: e.activation(oc[0][:, 5:6], oc[0][:, 4:5],
                                               AF.Ln,
                                               bias=self.c_eps[0][:, 0:1]),
                  reads=[oc[1], self.c_eps[1]], writes=[oc[1]])
            kb.op("AC", lambda e: e.activation(oc[0][:, 6:7], oc[0][:, 5:6],
                                               AF.Exp, scale=-0.5),
                  reads=[oc[1]], writes=[oc[1]])
            yield
            kb.op("DV", lambda e: e.tensor_scalar(on[0][:], otok[0][:],
                                                  oc[0][:, 2:3],
                                                  oc[0][:, 6:7],
                                                  op0=ALU.subtract,
                                                  op1=ALU.mult),
                  reads=[otok[1], oc[1]], writes=[on[1]])
            b6 = self.bank("F")
            kb.op("PE", lambda e, b=b6: e.transpose(
                b[0][:, :128], on[0][:], self.c_idf[0][:]),
                reads=[on[1], self.c_idf[1]], writes=[b6[1]])
            kb.op("DV", lambda e, b=b6: e.tensor_scalar(
                rhst[0][:], b[0][:, :128], self.c_lnw[0][:, :],
                self.c_lnb[0][:, :], op0=ALU.mult, op1=ALU.add),
                reads=[b6[1], self.c_lnw[1], self.c_lnb[1]],
                writes=[rhst[1]])
            kb.op("DV", lambda e, g=gT, tc=tc, h=h: e.tensor_tensor(
                self.A1[tc][0][:, h * 128:(h + 1) * 128], rhst[0][:],
                g[0], op=ALU.mult),
                reads=[rhst[1], gT[1]], writes=[self.A1[tc][1]])

        for hp in range(0, NH, 4):
            for q in range(4):
                load_head(hp + q, S[q])
            for tc in range(NCH):
                alive = [chunk2(hp + q, tc, S[q]) for q in range(4)]
                while alive:
                    nxt = []
                    for g in alive:
                        try:
                            next(g)
                            nxt.append(g)
                        except StopIteration:
                            pass
                    alive = nxt
        kb.barrier()
        st.close()

    def p7b_epilogue(self):
        kb = self.kb
        st = ExitStack()
        wo, wor = self.Wo
        for i in range(16):
            tokw = kb.dma("GP", self.A2[i][0][:], wo[i * 128:(i + 1) * 128, :],
                          reads=[wor], writes=[self.A2[i][1]], ch="wgt")
        for i in range(16):
            self.A2[i][1].prod = tokw
        xbf = [self.sbuf(f"exb{i}", [128, H], F32, st) for i in range(2)]
        x1c = self.sbuf("x1c", [128, H], F32, st)
        xn1c = self.sbuf("xn1c", [128, H], BF16, st)
        lnscr = self.sbuf("lnscr2", [128, H], F32, st)
        xr, xres = self.x_r
        x1t, x1r = self.x1_s
        xn1t, xn1r = self.xn1_s
        for tc in range(NCH):
            bx = xbf[tc % 2]
            kb.dma("SY", bx[0][:], xr[128 + tc * 128: 128 + (tc + 1) * 128, :],
                   reads=[xres], writes=[bx[1]], ch=f"x{tc % 2}")
            for dmg in range(4):
                bk = self.bank("F")
                for f in range(16):
                    kb.op("PE", lambda e, b=bk, f=f, tc=tc, dmg=dmg: e.matmul(
                        b[0][:, :], self.A1[tc][0][:, f * 128:(f + 1) * 128],
                        self.A2[f][0][:, dmg * 512:(dmg + 1) * 512],
                        start=(f == 0), stop=(f == 15)),
                        reads=[self.A1[tc][1], self.A2[f][1]],
                        writes=([bk[1]] if f in (0, 15) else []))
                kb.op("DV", lambda e, b=bk, bx=bx, dmg=dmg: e.tensor_tensor(
                    x1c[0][:, dmg * 512:(dmg + 1) * 512], b[0][:, :],
                    bx[0][:, dmg * 512:(dmg + 1) * 512], op=ALU.add),
                    reads=[bk[1], bx[1]], writes=[x1c[1]])
            self.emit_ln(x1c, xn1c, H, scr=lnscr)
            kb.dma("SY", x1t[tc], x1c[0][:], reads=[x1c[1]], writes=[x1r],
                   ch="sp_x1")
            kb.dma("SY", xn1t[tc], xn1c[0][:], reads=[xn1c[1]],
                   writes=[xn1r], ch="sp_xn1")
        kb.barrier()
        st.close()

    def p8_ffn(self):
        kb = self.kb
        st = ExitStack()
        xn1t, xn1r = self.xn1_s
        xbuf = [self.sbuf(f"fxb{i}", [128, H], BF16, st) for i in range(2)]
        for tc in range(NCH):
            bx = xbuf[tc % 2]
            kb.dma("SY", bx[0][:], xn1t[tc], reads=[xn1r], writes=[bx[1]],
                   ch=f"x{tc % 2}")
            for d in range(16):
                bt = self.bank("B")
                kb.op("PE", lambda e, b=bt, bx=bx, d=d: e.transpose(
                    b[0][:, :128], bx[0][:, d * 128:(d + 1) * 128],
                    self.c_idb[0][:]),
                    reads=[bx[1], self.c_idb[1]], writes=[bt[1]])
                kb.op("AC", lambda e, b=bt, d=d, tc=tc: e.copy(
                    self.A2[d][0][:, tc * 128:(tc + 1) * 128], b[0][:, :128]),
                    reads=[bt[1]], writes=[self.A2[d][1]])
        kb.barrier()
        self.psumB_stack.close()
        self.banksF.append(self.psum("pf6", [128, 512], F32, st))
        self.banksF.append(self.psum("pf7", [128, 512], F32, st))
        gate = [self.sbuf(f"gt{i}", [128, 512], BF16, st) for i in range(4)]
        fuse_st = [self.sbuf(f"fst{i}", [128, H], BF16, st) for i in range(2)]
        outb = [self.sbuf(f"ob{i}", [128, 1024], F32, st) for i in range(4)]
        x1p = [self.sbuf(f"fx1_{i}", [128, 1024], F32, st) for i in range(4)]
        oTt = [self.sbuf(f"oT{i}", [128, 512], F32, st) for i in range(4)]
        ftmp = self.sbuf("ftmp", [128, 512], F32, st)
        fuse_d = self.dram("fuse_d", [16, 128, H], BF16)
        wg8, wg8r = self.wg8
        wu8, wu8r = self.wu8
        wd8, wd8r = self.wd8
        x1t, x1r = self.x1_s
        outt, outres = self.out_r

        def wslot(b, kind, c):
            return self.A1[8 * b + 4 * kind + c // 4], (c % 4) * 512

        def load_wblk(tg, blk, b):
            tok = None
            for kind, (w8, w8r) in ((0, (wg8, wg8r)), (1, (wu8, wu8r))):
                for q in range(4):
                    t = self.A1[8 * b + 4 * kind + q]
                    tok = kb.dma("GP", t[0][:, 0:2048],
                                 w8[blk, :, q * 2048:(q + 1) * 2048],
                                 reads=[w8r], writes=[t[1]], ch=f"w8{b}")
            for kind in range(2):
                for q in range(4):
                    self.A1[8 * b + 4 * kind + q][1].prod = tok

        for tg in range(4):
            ts_ = slice(tg * 512, (tg + 1) * 512)
            if tg == 0:
                load_wblk(tg, 0, 0)
            for blk in range(16):
                b = blk % 2
                if blk < 15:
                    load_wblk(tg, blk + 1, 1 - b)
                for ic in range(4):
                    bk = self.banksF[ic]
                    for c in range(16):
                        t, off = wslot(b, 0, c)
                        kb.op("PE", lambda e, bk=bk, t=t, off=off, ic=ic,
                              ts_=ts_, c=c: e.matmul(
                            bk[0][:, :], t[0][:, off + ic * 128:off + (ic + 1) * 128],
                            self.A2[c][0][:, ts_], start=(c == 0),
                            stop=(c == 15)),
                            reads=[t[1], self.A2[c][1]],
                            writes=([bk[1]] if c in (0, 15) else []))
                    kb.op("AC", lambda e, bk=bk, ic=ic, blk=blk: e.activation(
                        gate[ic][0][:], bk[0][:, :], AF.Silu,
                        scale=self.c_sg[0][:, blk * 4 + ic:blk * 4 + ic + 1]),
                        reads=[bk[1], self.c_sg[1]], writes=[gate[ic][1]])
                fst = fuse_st[b]
                for ic in range(4):
                    bk = self.banksF[4 + ic % 4] if ic < 4 else None
                    bk = self.banksF[(4 + ic) % 8]
                    for c in range(16):
                        t, off = wslot(b, 1, c)
                        kb.op("PE", lambda e, bk=bk, t=t, off=off, ic=ic,
                              ts_=ts_, c=c: e.matmul(
                            bk[0][:, :], t[0][:, off + ic * 128:off + (ic + 1) * 128],
                            self.A2[c][0][:, ts_], start=(c == 0),
                            stop=(c == 15)),
                            reads=[t[1], self.A2[c][1]],
                            writes=([bk[1]] if c in (0, 15) else []))
                    kb.op("AC", lambda e, bk=bk, ic=ic, blk=blk: e.activation(
                        ftmp[0][:], bk[0][:, :], AF.Copy,
                        scale=self.c_su[0][:, blk * 4 + ic:blk * 4 + ic + 1]),
                        reads=[bk[1], self.c_su[1]], writes=[ftmp[1]])
                    kb.op("DV", lambda e, ic=ic, fst=fst: e.tensor_tensor(
                        fst[0][:, ic * 512:(ic + 1) * 512], ftmp[0][:],
                        gate[ic][0][:], op=ALU.mult),
                        reads=[ftmp[1], gate[ic][1]], writes=[fst[1]])
                kb.dma("SY", fuse_d[0][blk], fst[0][:], reads=[fst[1]],
                       writes=[fuse_d[1]], ch=f"sp_f{b}")
            for bb in range(2):
                c = kb.chan(f"sp_f{bb}")
                kb.SY.wait_tok((c, c.count))
            fuse_d[1].prod = None
            for dmh in range(2):
                for grp in range(8):
                    gp = grp % 2
                    for q in range(4):
                        t = self.A1[8 + gp * 4 + q]
                        kb.dma("GP", t[0][:, 0:2048],
                               wd8[dmh, grp, :, q * 2048:(q + 1) * 2048],
                               reads=[wd8r], writes=[t[1]],
                               ch=f"wd{gp * 4 + q}")
                    for q2 in range(2):
                        t = self.A1[2 * gp + q2]
                        kb.dma("SY", t[0][:, 0:2048],
                               fuse_d[0][grp * 2 + q2],
                               reads=[fuse_d[1]], writes=[t[1]],
                               ch=f"fl{gp * 2 + q2}")
                    for ic8 in range(8):
                        icg = grp * 8 + ic8
                        wt = self.A1[8 + gp * 4 + ic8 // 2]
                        woff = (ic8 % 2) * 1024
                        ft = self.A1[2 * gp + ic8 // 4]
                        foff = (ic8 % 4) * 512
                        for dmi in range(8):
                            bk = self.banksF[dmi]
                            kb.op("PE", lambda e, bk=bk, wt=wt, woff=woff,
                                  ft=ft, foff=foff, dmi=dmi, icg=icg:
                                  e.matmul(
                                bk[0][:, :],
                                wt[0][:, woff + dmi * 128:woff + (dmi + 1) * 128],
                                ft[0][:, foff:foff + 512],
                                start=(icg == 0), stop=(icg == 63)),
                                reads=[wt[1], ft[1]],
                                writes=([bk[1]] if icg in (0, 63) else []))
                for tt4 in range(4):
                    tcg = tg * 4 + tt4
                    kb.dma("SY", x1p[tt4][0][:],
                           x1t[tcg, :, dmh * 1024:(dmh + 1) * 1024],
                           reads=[x1r], writes=[x1p[tt4][1]], ch=f"x1p{tt4}")
                for half in range(2):
                    for di in range(4):
                        dmi = half * 4 + di
                        bk = self.banksF[dmi]
                        kb.op("AC", lambda e, bk=bk, ot=oTt[di], dmh=dmh,
                              dmi=dmi: e.activation(
                            ot[0][:], bk[0][:, :], AF.Copy,
                            scale=self.c_sd[0][:, dmh * 8 + dmi:
                                               dmh * 8 + dmi + 1]),
                            reads=[bk[1], self.c_sd[1]], writes=[oTt[di][1]])
                    for di in range(4):
                        dmi = half * 4 + di
                        ot = oTt[di]
                        for tt4 in range(4):
                            bkt = self.banksF[half * 4 + (di * 4 + tt4) % 2]
                            kb.op("PE", lambda e, b=bkt, ot=ot, tt4=tt4:
                                  e.transpose(
                                b[0][:, :128],
                                ot[0][:, tt4 * 128:(tt4 + 1) * 128],
                                self.c_idf[0][:]),
                                reads=[ot[1], self.c_idf[1]], writes=[bkt[1]])
                            kb.op("DV", lambda e, b=bkt, tt4=tt4, dmi=dmi:
                                  e.tensor_tensor(
                                outb[tt4][0][:, dmi * 128:(dmi + 1) * 128],
                                b[0][:, :128],
                                x1p[tt4][0][:, dmi * 128:(dmi + 1) * 128],
                                op=ALU.add),
                                reads=[bkt[1], x1p[tt4][1]],
                                writes=[outb[tt4][1]])
                for tt4 in range(4):
                    tcg = tg * 4 + tt4
                    kb.dma("SY",
                           outt[tcg * 128:(tcg + 1) * 128,
                                dmh * 1024:(dmh + 1) * 1024],
                           outb[tt4][0][:, :],
                           reads=[outb[tt4][1]], writes=[outres],
                           ch=f"ob{tt4}")
            if tg < 3:
                load_wblk(tg + 1, 0, 0)
        kb.barrier()
        st.close()

    def finish(self):
        kb = self.kb
        for name in self.taps:
            t, r = getattr(self, name)
            ot, orr = self.dram(f"tap_{name}", list(t.shape), t.dtype,
                                "ExternalOutput")
            kb.dma("SY", ot[:], t[:], reads=[r], writes=[orr], ch="tap")
        for c in kb.chans.values():
            if c.count:
                kb.SY.obj.wait_ge(c.sem, c.count)
        if kb.cc_count:
            kb.SY.obj.wait_ge(kb.CC, kb.cc_count)
        for e in (kb.PE, kb.DV, kb.AC):
            t = e.token()
            if t is not None:
                kb.SY.obj.wait_ge(e.sem, t[1])
        return self.nc


def kernel(**inp):
    out, _ = _run(inp, TAPS, STOP_AFTER)
    return out


def _run(inp, taps, stop_after):
    x = np.asarray(inp["L0_attn_x"])
    ident = np.eye(128, dtype=np.float32)
    shared = dict(
        Wq=np.ascontiguousarray(inp["L0_attn_Wq"], np.float32),
        Wk=np.ascontiguousarray(inp["L0_attn_Wk"], np.float32),
        Wv=np.ascontiguousarray(inp["L0_attn_Wv"], np.float32),
        Wg=np.ascontiguousarray(inp["L0_attn_Wg"], np.float32),
        Wo=np.ascontiguousarray(inp["L0_attn_Wo"], np.float32),
        WaT=np.ascontiguousarray(inp["L0_attn_Wa"].T, np.float32),
        WbT=np.ascontiguousarray(inp["L0_attn_Wb"].T, np.float32),
        ba_r=np.ascontiguousarray(inp["L0_attn_ba"].reshape(NH, 1), np.float32),
        bb_r=np.ascontiguousarray(inp["L0_attn_bb"].reshape(NH, 1), np.float32),
        qcw_r=np.ascontiguousarray(np.asarray(inp["L0_attn_qcw"])
                                   .reshape(16, 128, 4).transpose(1, 0, 2)
                                   .reshape(128, 64), np.float32),
        kcw_r=np.ascontiguousarray(np.asarray(inp["L0_attn_kcw"])
                                   .reshape(16, 128, 4).transpose(1, 0, 2)
                                   .reshape(128, 64), np.float32),
        vcw_r=np.ascontiguousarray(np.asarray(inp["L0_attn_vcw"])
                                   .reshape(16, 128, 4).transpose(1, 0, 2)
                                   .reshape(128, 64), np.float32),
        qcb_r=np.ascontiguousarray(np.asarray(inp["L0_attn_qcb"])
                                   .reshape(16, 128).T, np.float32),
        kcb_r=np.ascontiguousarray(np.asarray(inp["L0_attn_kcb"])
                                   .reshape(16, 128).T, np.float32),
        vcb_r=np.ascontiguousarray(np.asarray(inp["L0_attn_vcb"])
                                   .reshape(16, 128).T, np.float32),
        lnw_r=np.ascontiguousarray(inp["L0_attn_ln_w"].reshape(128, 1),
                                   np.float32),
        lnb_r=np.ascontiguousarray(inp["L0_attn_ln_b"].reshape(128, 1),
                                   np.float32),
        wg8=np.ascontiguousarray(np.asarray(inp["L0_ffn_w_gate"])
                                 .reshape(16, 128, 16, 512)
                                 .transpose(2, 1, 0, 3).reshape(16, 128, INTER)),
        wu8=np.ascontiguousarray(np.asarray(inp["L0_ffn_w_up"])
                                 .reshape(16, 128, 16, 512)
                                 .transpose(2, 1, 0, 3).reshape(16, 128, INTER)),
        wd8=np.ascontiguousarray(np.asarray(inp["L0_ffn_w_down"])
                                 .reshape(8, 8, 128, 2, 1024)
                                 .transpose(3, 0, 2, 1, 4)
                                 .reshape(2, 8, 128, 8192)),
        sg_r=np.ascontiguousarray(np.asarray(inp["L0_ffn_s_gate"])
                                  .reshape(64, 128).T, np.float32),
        su_r=np.ascontiguousarray(np.asarray(inp["L0_ffn_s_up"])
                                  .reshape(64, 128).T, np.float32),
        sd_r=np.ascontiguousarray(np.asarray(inp["L0_ffn_s_down"])
                                  .reshape(16, 128).T, np.float32),
        idf=ident,
        idb=ident.astype(bfdt),
        msu=np.ascontiguousarray(np.triu(np.ones((128, 128), np.float32), 1)),
        miu=np.ascontiguousarray(np.triu(np.ones((128, 128), np.float32), 0)),
        onesr=np.ones((65, 128), np.float32),
    )
    in_maps = []
    for c in range(8):
        bi, g = c // 4, c % 4
        rs = g * TPC
        halo = (x[bi, rs - 3: rs] if g > 0 else np.zeros((3, H), np.float32))
        x_r = np.ascontiguousarray(
            np.concatenate([np.zeros((125, H), np.float32), halo,
                            x[bi, rs: rs + TPC]], axis=0), np.float32)
        mj = np.zeros((128, 6), np.float32)
        for j in range(3):
            mj[:, j] = 1.0 if j < g else 0.0
            mj[:, 3 + j] = 0.0 if j < g else 1.0
        in_maps.append(dict(shared, x_r=x_r, mj_r=mj))
    b = Build(taps=taps, stop_after=stop_after)
    nc = b.build()
    res = run_bass_kernel_spmd(nc, in_maps, core_ids=list(range(8)))
    out = np.empty((B, S, H), np.float32)
    for c in range(8):
        bi, g = c // 4, c % 4
        out[bi, g * TPC:(g + 1) * TPC] = res.results[c]["out_r"]
    return out, res.results
